# version 11
# baseline (speedup 1.0000x reference)
import contextlib
import numpy as np
import ml_dtypes

import concourse.bass as bass
import concourse.mybir as mybir
from concourse.bass_utils import run_bass_kernel_spmd

F32 = mybir.dt.float32
BF16 = mybir.dt.bfloat16
AF = mybir.ActivationFunctionType
OP = mybir.AluOpType
AX = mybir.AxisListType

NCORES = 8
SEQS = 2
T = 2048
D = 1024
NT = T // 128
H = 16
KVH = 4
HD = 64
WIN = 512
DFF = 4096
MFF = DFF // 128
LNEPS = 1e-5
QKEPS = 1e-6
STAGE = 5
INTERLEAVE = True
NOPAIRS = False
NOKVQ = False
SUB = 99
QKMODE = 'full'

PERM = [0, 4, 1, 5, 2, 6, 3, 7, 8, 12, 9, 13, 10, 14, 11, 15]
GOF = [PERM[s] // 4 for s in range(H)]

W1_OFF = 0
W2_OFF = 32768
WKV_OFF = W2_OFF
WQ_OFF = WKV_OFF + 4096
WP_OFF = WQ_OFF + 8192
WARENA_N = 65536

ENGINES = ("pe", "act", "dve", "pool", "sync")
DMA_SEM = {"sync": "d_sync", "pool": "d_pool"}
CMP_SEM = {"pe": "c_pe", "act": "c_act", "dve": "c_dve", "pool": "c_pool"}


class TL:

    def __init__(self):
        self.ops = []

    def emit(self, eng, fn, reads=(), writes=(), dma=False, post_drain=False,
             dsem=None, dpolicy="slot"):
        op = {
            "eng": eng, "fn": fn, "reads": tuple(reads), "writes": tuple(writes),
            "dma": dma, "idx": len(self.ops), "ev": None, "need_ev": dma,
            "post_drain": post_drain, "dsem": dsem, "dpolicy": dpolicy,
        }
        self.ops.append(op)
        return op

    def finalize(self):
        last_w = {}
        last_r = {}
        for op in self.ops:
            deps = {}
            for t in op["reads"]:
                w = last_w.get(t)
                if w is not None:
                    deps[w["idx"]] = w
            for t in op["writes"]:
                w = last_w.get(t)
                if w is not None and not (w["dma"] and op["dma"]):
                    deps[w["idx"]] = w
                for r in last_r.get(t, {}).values():
                    deps[r["idx"]] = r
            op["deps"] = [self.ops[i] for i in sorted(deps)]
            for t in op["reads"]:
                last_r.setdefault(t, {})[op["eng"]] = op
            for t in op["writes"]:
                last_w[t] = op
                last_r[t] = {}
        for op in self.ops:
            for d in op["deps"]:
                if d["dma"] or d["eng"] != op["eng"] or op["eng"] != "pe":
                    d["need_ev"] = True
        counters = {}
        for op in self.ops:
            if op["dma"]:
                sem = op["dsem"]
                counters[sem] = counters.get(sem, 0) + 16
                op["ev"] = (sem, counters[sem])
            elif op["need_ev"]:
                sem = CMP_SEM[op["eng"]]
                counters[sem] = counters.get(sem, 0) + 1
                op["ev"] = (sem, counters[sem])
        for op in self.ops:
            if op["dma"] and op["dpolicy"] == "all":
                op["ev"] = (op["dsem"], counters[op["dsem"]])
        items = {e: [] for e in ENGINES}
        watermark = {}
        last_drain = {e: -1 for e in ENGINES}
        for op in self.ops:
            eng = op["eng"]
            waits = {}
            need_drain = False
            for d in op["deps"]:
                if d["eng"] == "pe" and eng == "pe" and not d["dma"] and not op["dma"]:
                    continue
                if d["eng"] == eng and not d["dma"]:
                    if d["idx"] > last_drain[eng]:
                        need_drain = True
                    continue
                sem, val = d["ev"]
                if waits.get(sem, 0) < val:
                    waits[sem] = val
            if need_drain:
                items[eng].append(("drain",))
                last_drain[eng] = op["idx"] - 1
            for sem, val in sorted(waits.items()):
                if watermark.get((eng, sem), 0) < val:
                    items[eng].append(("wait", sem, val))
                    watermark[(eng, sem)] = val
            items[eng].append(("op", op))
        self.counters = counters
        return items


def ceil_div(a, b):
    return -(-a // b)


def build_nc(nt=NT, seqs=SEQS):
    nc = bass.Bass("TRN2", target_bir_lowering=False)
    Tq = nt * 128

    x_d = nc.dram_tensor("x", [seqs, Tq, D], F32, kind="ExternalInput").ap()
    wq_d = nc.dram_tensor("wq", [128, 8192], F32, kind="ExternalInput").ap()
    wkv_d = nc.dram_tensor("wkv", [128, 4096], F32, kind="ExternalInput").ap()
    wp_d = nc.dram_tensor("wp", [128, 8192], F32, kind="ExternalInput").ap()
    w1_d = nc.dram_tensor("w1", [128, 32768], F32, kind="ExternalInput").ap()
    w2_d = nc.dram_tensor("w2", [128, 32768], F32, kind="ExternalInput").ap()
    b1_d = nc.dram_tensor("b1f", [128, MFF], F32, kind="ExternalInput").ap()
    b2_d = nc.dram_tensor("b2bc", [1, D], F32, kind="ExternalInput").ap()
    cos_d = nc.dram_tensor("cost", [128, nt * HD], F32, kind="ExternalInput").ap()
    sin_d = nc.dram_tensor("sint", [128, nt * HD], F32, kind="ExternalInput").ap()
    ident_d = nc.dram_tensor("ident", [128, 128], BF16, kind="ExternalInput").ap()
    selmsk_d = nc.dram_tensor("selmsk", [128, 6 * 128], BF16, kind="ExternalInput").ap()
    x1_d = nc.dram_tensor("x1s", [seqs, Tq, D], F32, kind="Internal").ap()
    y_d = nc.dram_tensor("y", [seqs, Tq, D], F32, kind="ExternalOutput").ap()

    tl = TL()

    with contextlib.ExitStack() as ctx:
        sb = lambda name, shape, dt: ctx.enter_context(nc.sbuf_tensor("sb_" + name, shape, dt))
        ps = lambda name, shape, dt: ctx.enter_context(nc.psum_tensor(name, shape, dt))

        warena = sb("warena", [128, WARENA_N], BF16)
        cost = sb("cost", [128, nt * HD], BF16)
        sint = sb("sint", [128, nt * HD], BF16)
        b1f = sb("b1f", [128, MFF], F32)
        ident = sb("ident", [128, 128], BF16)
        selmsk = sb("selmsk", [128, 6 * 128], BF16)
        xt = [sb("xt0", [128, D], F32), sb("xt1", [128, D], F32)]
        x1t = [sb("x1t0", [128, D], F32), sb("x1t1", [128, D], F32)]
        junk = sb("junk", [128, D], BF16)
        sqr = sb("sqr", [128, D], BF16)
        lnout = sb("lnout", [128, D], F32)
        ao = sb("ao", [128, D], F32)
        ln1 = sb("ln1", [128, D], BF16)
        lt = [sb("lt0", [128, D], BF16), sb("lt1", [128, D], BF16)]
        qf = sb("qf", [128, D], BF16)
        qr = sb("qr", [128, D], BF16)
        qT = [sb("qT0", [128, D], BF16), sb("qT1", [128, D], BF16)]
        kf = sb("kf", [128, 256], BF16)
        kr4 = sb("kr4", [128, 256], BF16)
        krb = sb("krb", [128, 256], BF16)
        KT = sb("KT", [128, 6 * 256], BF16)
        VR = sb("VR", [128, 6 * 256], BF16)
        _a0 = [sb("aT00", [128, 640], BF16), sb("aT01", [128, 640], BF16)]
        Eb = [[sb("E00", [128, 640], BF16), sb("E01", [128, 640], BF16)],
              [sb("E10", [128, 640], BF16), sb("E11", [128, 640], BF16)]]
        aT = [_a0, _a0]
        yT = sb("yT", [128, D], BF16)
        _h0 = sb("hT0", [128, DFF], BF16)
        hT = [_h0, _h0]
        ff = sb("ff", [128, D], F32)
        st = sb("st", [128, 128], F32)
        b2r = sb("b2r", [1, D], BF16)
        onesr = sb("onesr", [1, 128], BF16)
        epsc = sb("epsc", [128, 1], F32)

        pjq0 = ps("pjq0", [128, 512], F32)
        pjq1 = ps("pjq1", [128, 512], F32)
        pjk = ps("pjk", [128, 512], F32)
        psE = ps("psE", [128, 512], F32)
        psO = ps("psO", [128, 512], F32)
        psAV = ps("psAV", [128, 512], F32)
        psX = ps("psX", [128, 512], F32)
        ptr = ps("ptr", [128, 1024], BF16)

        sems = {}
        for s in ("c_pe", "c_act", "c_dve", "c_pool", "d_wc", "d_w1", "d_w2",
                  "d_xl0", "d_xl1", "d_x1l0", "d_x1l1", "d_x1w0", "d_x1w1",
                  "d_yw0", "d_yw1"):
            sems[s] = ctx.enter_context(nc.semaphore(s))

        E = tl.emit

        SETBASE = {"A0": 0, "A1": 8, "B0": 16, "B1": 24}
        LNC = {"s1": 0, "s2": 1, "nm": 2, "t2": 3, "lnv": 4, "rstd": 5}

        def lnc(set_, name):
            return st[:, SETBASE[set_] + LNC[name]:SETBASE[set_] + LNC[name] + 1]

        def qkc(p, name, w=1):
            base = 32 + 40 * p + {"qss": 0, "qn": 16, "kss": 32, "kn": 36}[name]
            return st[:, base:base + w]

        def dnc(cp, i, name, w=1):
            base = 112 + cp * 8 + {"da": 0, "db": 2, "den": 4, "rc": 6,
                                   "lnb": 6}[name] + i
            return st[:, base:base + w]

        E("dve", lambda e: e.memset(epsc[:], LNEPS), writes=("epsc",))
        E("dve", lambda e: e.memset(onesr[:], 1.0), writes=("onesr",))
        E("pool", lambda e: e.dma_start(out=ident[:], in_=ident_d), writes=("ident",), dma=True, dsem="d_wc", dpolicy="all")
        E("pool", lambda e: e.dma_start(out=selmsk[:], in_=selmsk_d), writes=("selmsk",), dma=True, dsem="d_wc", dpolicy="all")
        E("pool", lambda e: e.dma_start(out=cost[:], in_=cos_d), writes=("cost",), dma=True, dsem="d_wc", dpolicy="all")
        E("pool", lambda e: e.dma_start(out=sint[:], in_=sin_d), writes=("sint",), dma=True, dsem="d_wc", dpolicy="all")
        E("pool", lambda e: e.dma_start(out=b1f[:], in_=b1_d), writes=("b1f",), dma=True, dsem="d_wc", dpolicy="all")
        E("pool", lambda e: e.dma_start(out=b2r[:], in_=b2_d), writes=("b2r",), dma=True, dsem="d_wc", dpolicy="all")
        E("pool", lambda e: e.dma_start(out=warena[:, WKV_OFF:WKV_OFF + 4096], in_=wkv_d),
          writes=("w_kv", "w_w2_0"), dma=True, dsem="d_wc", dpolicy="all")
        E("pool", lambda e: e.dma_start(out=warena[:, WQ_OFF:WQ_OFF + 8192], in_=wq_d),
          writes=("w_q", "w_w2_0", "w_w2_1"), dma=True, dsem="d_wc", dpolicy="all")
        E("pool", lambda e: e.dma_start(out=warena[:, WP_OFF:WP_OFF + 8192], in_=wp_d),
          writes=("w_proj", "w_w2_1", "w_w2_2"), dma=True, dsem="d_wc", dpolicy="all")
        for i in range(4):
            E("pool", lambda e, i=i: e.dma_start(
                out=warena[:, W1_OFF + i * 8192:W1_OFF + (i + 1) * 8192],
                in_=w1_d[:, i * 8192:(i + 1) * 8192]), writes=(f"w_w1_{i}",),
              dma=True, dsem="d_w1", dpolicy="all")

        def ln_stats(src_tag, src_ap, set_):
            tg = lambda n: f"st_{set_}_{n}"
            E("act", lambda e: e.activation(junk[:], src_ap, AF.Square, scale=1.0 / 32.0,
                                            accum_out=lnc(set_, "s2")),
              reads=(src_tag,), writes=(tg("s2"), "jkJ", "jkS"))
            E("dve", lambda e: e.tensor_reduce(lnc(set_, "s1"), src_ap, AX.X, OP.add),
              reads=(src_tag,), writes=(tg("s1"),))
            E("dve", lambda e: e.tensor_scalar(lnc(set_, "nm"), lnc(set_, "s1"),
                                               -1.0 / D, None, OP.mult),
              reads=(tg("s1"),), writes=(tg("nm"),))
            E("dve", lambda e: e.scalar_tensor_tensor(
                lnc(set_, "t2"), lnc(set_, "nm"), lnc(set_, "nm"), lnc(set_, "s2"),
                OP.mult, OP.subtract),
              reads=(tg("nm"), tg("s2")), writes=(tg("t2"),))
            E("act", lambda e: e.activation(lnc(set_, "lnv"), lnc(set_, "t2"), AF.Ln,
                                            scale=-1.0, bias=epsc[:]),
              reads=(tg("t2"), "epsc"), writes=(tg("lnv"),))
            E("act", lambda e: e.activation(lnc(set_, "rstd"), lnc(set_, "lnv"),
                                            AF.Exp, scale=-0.5),
              reads=(tg("lnv"),), writes=(tg("rstd"),))

        def ln_norm(src_tag, src_ap, out_tag, out_ap, set_):
            E("dve", lambda e: e.tensor_scalar(out_ap, src_ap, lnc(set_, "nm"),
                                               lnc(set_, "rstd"), OP.add, OP.mult),
              reads=(src_tag, f"st_{set_}_nm", f"st_{set_}_rstd"), writes=(out_tag,))

        def transposes(src_tag, src_ap_fn, nchunks, dst_tag, dst_sb):
            for c in range(nchunks):
                E("pe", lambda e, c=c: e.transpose(
                    ptr[:, c * 128:(c + 1) * 128], src_ap_fn(c), ident[:]),
                  reads=(src_tag, "ident"), writes=("ptr",))
            off = 0
            while off < nchunks * 128:
                w = min(512, nchunks * 128 - off)
                E("dve", lambda e, off=off, w=w: e.tensor_copy(
                    dst_sb[:, off:off + w], ptr[:, off:off + w]),
                  reads=("ptr",), writes=(dst_tag,))
                off += w

        def ln1_A(s, t, glob):
            p = glob % 2
            ln_stats(f"xt{p}", xt[p][:], f"A{p}")
            ln_norm(f"xt{p}", xt[p][:], "ln1", ln1[:], f"A{p}")
            transposes("ln1", lambda c: ln1[:, c * 128:(c + 1) * 128], 8,
                       f"lt{p}", lt[p])

        def kvq_A(s, t, glob):
            p = glob % 2
            ltb, lttag = lt[p], f"lt{p}"
            qTb, qTtag = qT[p], f"qT{p}"
            slot = glob % 6
            cslice = lambda j0, w: cost[:, t * HD + j0: t * HD + j0 + w]
            sslice = lambda j0, w: sint[:, t * HD + j0: t * HD + j0 + w]

            def kv_mms():
                for c in range(8):
                    E("pe", lambda e, c=c: e.matmul(
                        pjq1[:, :], ltb[:, c * 128:(c + 1) * 128],
                        warena[:, WKV_OFF + c * 512:WKV_OFF + (c + 1) * 512],
                        start=(c == 0), stop=(c == 7)),
                      reads=(lttag, "w_kv"), writes=("pjq1",))

            def k_chain():
                E("act", lambda e: e.activation(sqr[:, 0:256], pjq1[:, 0:256], AF.Square),
                  reads=("pjq1",), writes=("sqr",))
                E("dve", lambda e: e.tensor_reduce(
                    qkc(p, "kss", 4), sqr[:, 0:256].rearrange("p (g w) -> p g w", g=4),
                    AX.X, OP.add),
                  reads=("sqr",), writes=(f"st_kss{p}",))
                E("act", lambda e: e.activation(qkc(p, "kss", 4), qkc(p, "kss", 4), AF.Ln),
                  reads=(f"st_kss{p}",), writes=(f"st_kss{p}",))
                E("act", lambda e: e.activation(qkc(p, "kss", 4), qkc(p, "kss", 4),
                                                AF.Exp, scale=0.5),
                  reads=(f"st_kss{p}",), writes=(f"st_kss{p}",))
                E("dve", lambda e: e.tensor_scalar(qkc(p, "kss", 4), qkc(p, "kss", 4),
                                                   QKEPS, None, OP.add),
                  reads=(f"st_kss{p}",), writes=(f"st_kss{p}",))
                E("dve", lambda e: e.reciprocal(qkc(p, "kss", 4), qkc(p, "kss", 4)),
                  reads=(f"st_kss{p}",), writes=(f"st_kss{p}",))
                E("dve", lambda e: e.tensor_tensor(
                    kf[:].rearrange("p (g w) -> p g w", g=4),
                    pjq1[:, 0:256].rearrange("p (g w) -> p g w", g=4),
                    qkc(p, "kss", 4).unsqueeze(2).broadcast_to((128, 4, HD)), OP.mult),
                  reads=("pjq1", f"st_kss{p}"), writes=("kf",))
                E("dve", lambda e: e.tensor_copy(
                    VR[:, slot * 256:slot * 256 + 256], pjq1[:, 256:512]),
                  reads=("pjq1",), writes=(f"V{slot}",))
                E("dve", lambda e: e.tensor_tensor(
                    kr4[:].rearrange("p (g w) -> p g w", g=4)[:, :, 0:32],
                    kf[:].rearrange("p (g w) -> p g w", g=4)[:, :, 32:64],
                    sslice(0, 32).unsqueeze(1).broadcast_to((128, 4, 32)), OP.mult),
                  reads=("kf", "sint"), writes=("kr4",))
                E("dve", lambda e: e.tensor_tensor(
                    kr4[:].rearrange("p (g w) -> p g w", g=4)[:, :, 32:64],
                    kf[:].rearrange("p (g w) -> p g w", g=4)[:, :, 0:32],
                    sslice(32, 32).unsqueeze(1).broadcast_to((128, 4, 32)), OP.mult),
                  reads=("kf", "sint"), writes=("kr4",))
                E("dve", lambda e: e.tensor_tensor(
                    krb[:].rearrange("p (g w) -> p g w", g=4),
                    kf[:].rearrange("p (g w) -> p g w", g=4),
                    cslice(0, HD).unsqueeze(1).broadcast_to((128, 4, HD)), OP.mult),
                  reads=("kf", "cost"), writes=("krb",))
                E("dve", lambda e: e.tensor_add(krb[:], krb[:], kr4[:]),
                  reads=("krb", "kr4"), writes=("krb",))

            def kT_tr():
                for c in range(2):
                    E("pe", lambda e, c=c: e.transpose(
                        ptr[:, c * 128:(c + 1) * 128], krb[:, c * 128:(c + 1) * 128],
                        ident[:]),
                      reads=("krb", "ident"), writes=("ptr",))
                E("dve", lambda e: e.tensor_copy(
                    KT[:, slot * 256:slot * 256 + 256], ptr[:, 0:256]),
                  reads=("ptr",), writes=(f"K{slot}",))

            def q_half(half):
                import os
                q0mode = os.environ.get("Q0MODE", "full")
                for c in range(8):
                    E("pe", lambda e, c=c, half=half: e.matmul(
                        pjq0[:, :], ltb[:, c * 128:(c + 1) * 128],
                        warena[:, WQ_OFF + c * 1024 + half * 512:
                               WQ_OFF + c * 1024 + half * 512 + 512],
                        start=(c == 0), stop=(c == 7)),
                      reads=(lttag, "w_q"), writes=("pjq0",))
                if q0mode == "mmonly":
                    return
                E("act", lambda e, half=half: e.activation(
                    sqr[:, half * 512:half * 512 + 512], pjq0[:, :], AF.Square),
                  reads=("pjq0",), writes=("sqr", "q0tok"))
                if q0mode == "nocopy":
                    return
                E("dve", lambda e, half=half: e.tensor_copy(
                    qf[:, half * 512:half * 512 + 512], pjq0[:, :]),
                  reads=("pjq0", "q0tok"), writes=("qf",))

            def q_chain():
                E("dve", lambda e: e.tensor_reduce(
                    qkc(p, "qn", 16), sqr[:].rearrange("p (g w) -> p g w", g=16),
                    AX.X, OP.add),
                  reads=("sqr",), writes=(f"st_qn{p}",))
                E("act", lambda e: e.activation(qkc(p, "qn", 16), qkc(p, "qn", 16), AF.Ln),
                  reads=(f"st_qn{p}",), writes=(f"st_qn{p}",))
                E("act", lambda e: e.activation(qkc(p, "qn", 16), qkc(p, "qn", 16),
                                                AF.Exp, scale=0.5),
                  reads=(f"st_qn{p}",), writes=(f"st_qn{p}",))
                E("dve", lambda e: e.tensor_scalar(qkc(p, "qn", 16), qkc(p, "qn", 16),
                                                   QKEPS, None, OP.add),
                  reads=(f"st_qn{p}",), writes=(f"st_qn{p}",))
                E("dve", lambda e: e.reciprocal(qkc(p, "qn", 16), qkc(p, "qn", 16)),
                  reads=(f"st_qn{p}",), writes=(f"st_qn{p}",))
                E("dve", lambda e: e.tensor_tensor(
                    qr[:].rearrange("p (g w) -> p g w", g=16),
                    qf[:].rearrange("p (g w) -> p g w", g=16),
                    qkc(p, "qn", 16).unsqueeze(2).broadcast_to((128, 16, HD)), OP.mult),
                  reads=("qf", f"st_qn{p}"), writes=("qr",))
                E("dve", lambda e: e.tensor_tensor(
                    sqr[:].rearrange("p (g w) -> p g w", g=16)[:, :, 0:32],
                    qr[:].rearrange("p (g w) -> p g w", g=16)[:, :, 32:64],
                    sslice(0, 32).unsqueeze(1).broadcast_to((128, 16, 32)), OP.mult),
                  reads=("qr", "sint"), writes=("sqr",))
                E("dve", lambda e: e.tensor_tensor(
                    sqr[:].rearrange("p (g w) -> p g w", g=16)[:, :, 32:64],
                    qr[:].rearrange("p (g w) -> p g w", g=16)[:, :, 0:32],
                    sslice(32, 32).unsqueeze(1).broadcast_to((128, 16, 32)), OP.mult),
                  reads=("qr", "sint"), writes=("sqr",))
                E("dve", lambda e: e.tensor_tensor(
                    qf[:].rearrange("p (g w) -> p g w", g=16),
                    qr[:].rearrange("p (g w) -> p g w", g=16),
                    cslice(0, HD).unsqueeze(1).broadcast_to((128, 16, HD)), OP.mult),
                  reads=("qr", "cost"), writes=("qf",))
                E("dve", lambda e: e.tensor_add(qr[:], qf[:], sqr[:]),
                  reads=("qf", "sqr"), writes=("qr",))

            def qT_tr():
                transposes("qr", lambda c: qr[:, c * 128:(c + 1) * 128], 8, qTtag, qTb)

            import os
            mask = int(os.environ.get("KVQMASK", "127"))
            chunks = [kv_mms, k_chain, kT_tr, lambda: q_half(0), lambda: q_half(1),
                      q_chain, qT_tr]
            return [c for i, c in enumerate(chunks) if mask & (1 << i)]

        def tile_A(s, t, glob, nxt=None):
            p = glob % 2
            xtb, xtag = xt[p], f"xt{p}"
            x1b, x1tag = x1t[p], f"x1t{p}"
            qTb, qTtag = qT[p], f"qT{p}"
            setB = f"B{p}"

            def fallback_store():
                E("dve", lambda e: e.tensor_copy(x1b[:], xtb[:]),
                  reads=(xtag,), writes=(x1tag,))
                E("sync", lambda e: e.dma_start(
                    out=x1_d[s, t * 128:(t + 1) * 128, :], in_=x1b[:]),
                  reads=(x1tag,), writes=(f"x1s_{s}_{t}",), dma=True,
                  dsem=f"d_x1w{glob % 2}")

            if STAGE == 0 or STAGE in (45, 46, 47):
                fallback_store()
                return

            if nxt is not None:
                ln1_A(*nxt)
            kvq = kvq_A(*nxt) if nxt is not None else []

            j0 = max(0, t - 4)
            J = list(range(j0, t))
            na = len(J)
            mask_old = (t >= 4)

            def pair_phase1(c):
                cp = c % 2
                s0, s1_ = 2 * c, 2 * c + 1
                heads = []
                for i, (sl, bank, btag, base) in enumerate(
                        ((s0, psE, "psE", 0), (s1_, psO, "psO", 64))):
                    sbank, sbtag = (pjk, "pjk") if i == 0 else (psX, "psX")
                    heads.append(dict(
                        i=i, sl=sl, bank=bank, btag=btag, base=base, g=GOF[sl],
                        sbank=sbank, sbtag=sbtag, soff=cp * 128,
                        qap=qTb[base:base + 64, (sl // 2) * 128:(sl // 2) * 128 + 128],
                        eb=Eb[cp][i], etag=f"E{cp}{i}"))

                def kslice(hd, j):
                    sl6 = (glob - (t - j)) % 6
                    return KT[hd["base"]:hd["base"] + 64,
                              sl6 * 256 + (hd["g"] // 2) * 128:
                              sl6 * 256 + (hd["g"] // 2) * 128 + 128]

                for jj, j in enumerate(J):
                    for hd in heads:
                        bank, btag, qap = hd["bank"], hd["btag"], hd["qap"]
                        kap = kslice(hd, j)
                        if jj == 0 and mask_old:
                            E("pe", lambda e, bank=bank, qap=qap, kap=kap, jj=jj: e.matmul(
                                bank[:, jj * 128:(jj + 1) * 128], qap, kap,
                                start=True, stop=False),
                              reads=(qTtag, f"K{(glob - (t - j)) % 6}"), writes=(btag,))
                            for r in range(2):
                                selap = selmsk[hd["base"]:hd["base"] + 64,
                                               r * 128:(r + 1) * 128]
                                mskap = selmsk[hd["base"]:hd["base"] + 64,
                                               (4 + r) * 128:(5 + r) * 128]
                                E("pe", lambda e, bank=bank, selap=selap, mskap=mskap,
                                  jj=jj, r=r: e.matmul(
                                      bank[:, jj * 128:(jj + 1) * 128], selap, mskap,
                                      start=False, stop=(r == 1)),
                                  reads=("selmsk",), writes=(btag,))
                        else:
                            E("pe", lambda e, bank=bank, qap=qap, kap=kap, jj=jj: e.matmul(
                                bank[:, jj * 128:(jj + 1) * 128], qap, kap,
                                start=True, stop=True),
                              reads=(qTtag, f"K{(glob - (t - j)) % 6}"), writes=(btag,))
                for hd in heads:
                    sbank, sbtag, qap = hd["sbank"], hd["sbtag"], hd["qap"]
                    so = hd["soff"]
                    kap = kslice(hd, t)
                    E("pe", lambda e, sbank=sbank, qap=qap, kap=kap, so=so: e.matmul(
                        sbank[:, so:so + 128], qap, kap, start=True, stop=False),
                      reads=(qTtag, f"K{glob % 6}"), writes=(sbtag,))
                    for r in range(2):
                        selap = selmsk[hd["base"]:hd["base"] + 64, r * 128:(r + 1) * 128]
                        mskap = selmsk[hd["base"]:hd["base"] + 64,
                                       (2 + r) * 128:(3 + r) * 128]
                        E("pe", lambda e, sbank=sbank, selap=selap, mskap=mskap, r=r,
                          so=so: e.matmul(sbank[:, so:so + 128], selap, mskap,
                                          start=False, stop=(r == 1)),
                          reads=("selmsk",), writes=(sbtag,))
                for hd in heads:
                    i = hd["i"]
                    if na > 0:
                        E("act", lambda e, hd=hd, cp=cp: e.activation(
                            junk[:, 0:na * 128], hd["bank"][:, 0:na * 128],
                            AF.Exp, scale=0.125, accum_out=dnc(cp, hd["i"], "da")),
                          reads=(hd["btag"],), writes=(f"st_da{cp}{i}", "jkJ"))
                    E("act", lambda e, hd=hd, cp=cp: e.activation(
                        junk[:, 512:640], hd["sbank"][:, hd["soff"]:hd["soff"] + 128],
                        AF.Exp, scale=0.125, accum_out=dnc(cp, hd["i"], "db")),
                      reads=(hd["sbtag"],), writes=(f"st_db{cp}{i}", "jkS"))
                if na > 0:
                    E("dve", lambda e, cp=cp: e.tensor_tensor(
                        dnc(cp, 0, "den", 2), dnc(cp, 0, "da", 2),
                        dnc(cp, 0, "db", 2), OP.add),
                      reads=(f"st_da{cp}0", f"st_da{cp}1", f"st_db{cp}0",
                             f"st_db{cp}1"),
                      writes=(f"st_den{cp}0", f"st_den{cp}1"))
                else:
                    E("dve", lambda e, cp=cp: e.tensor_copy(
                        dnc(cp, 0, "den", 2), dnc(cp, 0, "db", 2)),
                      reads=(f"st_db{cp}0", f"st_db{cp}1"),
                      writes=(f"st_den{cp}0", f"st_den{cp}1"))
                E("dve", lambda e, cp=cp: e.reciprocal(
                    dnc(cp, 0, "rc", 2), dnc(cp, 0, "den", 2)),
                  reads=(f"st_den{cp}0", f"st_den{cp}1"),
                  writes=(f"st_rc{cp}0", f"st_rc{cp}1"))
                E("act", lambda e, cp=cp: e.activation(
                    dnc(cp, 0, "lnb", 2), dnc(cp, 0, "rc", 2), AF.Ln),
                  reads=(f"st_rc{cp}0", f"st_rc{cp}1"),
                  writes=(f"st_rc{cp}0", f"st_rc{cp}1"))
                for hd in heads:
                    i = hd["i"]
                    if na > 0:
                        E("act", lambda e, hd=hd, cp=cp: e.activation(
                            hd["eb"][:, 0:na * 128], hd["bank"][:, 0:na * 128],
                            AF.Exp, scale=0.125, bias=dnc(cp, hd["i"], "lnb")),
                          reads=(hd["btag"], f"st_rc{cp}{i}"), writes=(hd["etag"],))
                    E("act", lambda e, hd=hd, cp=cp: e.activation(
                        hd["eb"][:, na * 128:(na + 1) * 128],
                        hd["sbank"][:, hd["soff"]:hd["soff"] + 128],
                        AF.Exp, scale=0.125, bias=dnc(cp, hd["i"], "lnb")),
                      reads=(hd["sbtag"], f"st_rc{cp}{i}"), writes=(hd["etag"],))
                return heads

            def pair_phase2(c, heads):
                cp = c % 2
                nch = na + 1
                ev_slots = [0, 1, 2, 3, 4][:nch]
                od_slots = [5, 6, 7, 0, 1][:nch]
                for i, slots in ((0, ev_slots), (1, od_slots)):
                    hd = heads[i]
                    eb, etag = hd["eb"], hd["etag"]
                    atb, atag = aT[cp][i], f"aT{i}"
                    for jj, slot in enumerate(slots):
                        E("pe", lambda e, eb=eb, jj=jj, slot=slot: e.transpose(
                            ptr[:, slot * 128:(slot + 1) * 128],
                            eb[:, jj * 128:(jj + 1) * 128], ident[:]),
                          reads=(etag, "ident"), writes=("ptr",))
                    run = []
                    for jj, slot in enumerate(slots + [None]):
                        if run and (slot is None or slot != run[-1][1] + 1):
                            j0r, s0r = run[0]
                            w = len(run) * 128
                            E("dve", lambda e, atb=atb, j0r=j0r, s0r=s0r, w=w:
                              e.tensor_copy(atb[:, j0r * 128:j0r * 128 + w],
                                            ptr[:, s0r * 128:s0r * 128 + w]),
                              reads=("ptr",), writes=(atag,))
                            run = []
                        if slot is not None:
                            run.append((jj, slot))
                s0, s1_ = 2 * c, 2 * c + 1
                for i, sl in enumerate((s0, s1_)):
                    g = GOF[sl]
                    hd = heads[i]
                    atb, atag = aT[cp][i], f"aT{i}"
                    outap = psAV[i * 64:(i + 1) * 64,
                                 cp * 256 + i * 128:cp * 256 + (i + 1) * 128]
                    chunks = J + [t]
                    for jj, j in enumerate(chunks):
                        sl6 = (glob - (t - j)) % 6
                        vap = VR[:, sl6 * 256 + g * 64:sl6 * 256 + g * 64 + 64]
                        E("pe", lambda e, outap=outap, vap=vap, atb=atb, jj=jj, i=i,
                          first=(jj == 0), last=(jj == len(chunks) - 1): e.matmul(
                              outap, vap, atb[:, jj * 128:(jj + 1) * 128],
                              start=first, stop=last, tile_position=(0, i * 64)),
                          reads=(f"V{sl6}", atag), writes=("psAV",))
                E("dve", lambda e, c=c, cp=cp: e.tensor_copy(
                    yT[0:64, c * 128:(c + 1) * 128], psAV[0:64, cp * 256:cp * 256 + 128]),
                  reads=("psAV",), writes=("yT",))
                E("dve", lambda e, c=c, cp=cp: e.tensor_copy(
                    yT[64:128, c * 128:(c + 1) * 128],
                    psAV[64:128, cp * 256 + 128:cp * 256 + 256]),
                  reads=("psAV",), writes=("yT",))

            if NOKVQ:
                kvq = []
            sched = {1: [0], 2: [1], 3: [3], 4: [4], 5: [5], 6: [2]} if INTERLEAVE else {}
            prev = None
            if not NOPAIRS:
                for c in range(8):
                    heads = pair_phase1(c)
                    if prev is not None:
                        pair_phase2(prev[0], prev[1])
                    for k in sched.get(c, []):
                        if k < len(kvq):
                            kvq[k]()
                    prev = (c, heads)
                pair_phase2(prev[0], prev[1])
                if INTERLEAVE:
                    if len(kvq) > 6:
                        kvq[6]()
                else:
                    for chunk in kvq:
                        chunk()
            else:
                for chunk in kvq:
                    chunk()
                E("dve", lambda e: e.memset(yT[:], 0.0), writes=("yT",))

            if STAGE == 3:
                fallback_store()
                return

            for half, bank, btag in ((0, pjq0, "pjq0"), (1, pjq1, "pjq1")):
                for c in range(8):
                    E("pe", lambda e, c=c, half=half, bank=bank: e.matmul(
                        bank[:, :], yT[:, c * 128:(c + 1) * 128],
                        warena[:, WP_OFF + c * 1024 + half * 512:
                               WP_OFF + c * 1024 + half * 512 + 512],
                        start=(c == 0), stop=(c == 7)),
                      reads=("yT", "w_proj"), writes=(btag,))
            E("dve", lambda e: e.tensor_copy(ao[:, 0:512], pjq0[:, :]),
              reads=("pjq0",), writes=("ao",))
            E("act", lambda e: e.activation(ao[:, 512:1024], pjq1[:, :], AF.Copy),
              reads=("pjq1",), writes=("ao",))
            ln_stats("ao", ao[:], setB)
            ln_norm("ao", ao[:], "lnout", lnout[:], setB)
            E("dve", lambda e, x1b=x1b, xtb=xtb: e.tensor_add(x1b[:], lnout[:], xtb[:]),
              reads=("lnout", xtag), writes=(x1tag,))
            E("sync", lambda e, s=s, t=t, x1b=x1b: e.dma_start(
                out=x1_d[s, t * 128:(t + 1) * 128, :], in_=x1b[:]),
              reads=(x1tag,), writes=(f"x1s_{s}_{t}",), dma=True,
              dsem=f"d_x1w{glob % 2}")

        def ln3_B(s, t, glob):
            p = glob % 2
            ln_stats(f"x1t{p}", x1t[p][:], f"A{p}")
            ln_norm(f"x1t{p}", x1t[p][:], "ln1", ln1[:], f"A{p}")
            transposes("ln1", lambda c: ln1[:, c * 128:(c + 1) * 128], 8,
                       f"lt{p}", lt[p])

        def tile_B(s, t, glob, nxt=None):
            p = glob % 2
            x1b, x1tag = x1t[p], f"x1t{p}"
            otb, otag = xt[p], f"xt{p}"
            ltb, lttag = lt[p], f"lt{p}"
            hTb, hTt = hT[p], "hT"
            setA, setB = f"A{p}", f"B{p}"

            if STAGE < 5 and STAGE not in (45, 46, 47):
                E("dve", lambda e: e.tensor_copy(otb[:], x1b[:]),
                  reads=(x1tag,), writes=(otag,))
                E("sync", lambda e, s=s, t=t, otb=otb: e.dma_start(
                    out=y_d[s, t * 128:(t + 1) * 128, :], in_=otb[:]),
                  reads=(otag,), writes=(f"y_{s}_{t}",), dma=True,
                  dsem=f"d_yw{glob % 2}")
                return

            if glob == 0:
                ln3_B(s, t, glob)

            def trelu(m):
                slot = m % 8
                E("pe", lambda e, m=m, slot=slot: e.transpose(
                    ptr[:, slot * 128:(slot + 1) * 128],
                    hTb[:, m * 128:(m + 1) * 128], ident[:]),
                  reads=(f"{hTt}_{m}", "ident"), writes=("ptr",))
                E("dve", lambda e, m=m, slot=slot: e.tensor_scalar(
                    hTb[:, m * 128:(m + 1) * 128], ptr[:, slot * 128:(slot + 1) * 128],
                    b1f[:, m:m + 1], 0.0, OP.add, OP.max),
                  reads=("ptr", "b1f"), writes=(f"{hTt}_{m}",))

            for n in range(8):
                bank, btag = [(pjq0, "pjq0"), (pjq1, "pjq1"), (psE, "psE"),
                              (pjk, "pjk")][n % 4]
                for c in range(8):
                    blk = c * 8 + n
                    E("pe", lambda e, c=c, n=n, bank=bank, blk=blk: e.matmul(
                        bank[:, :], ltb[:, c * 128:(c + 1) * 128],
                        warena[:, W1_OFF + blk * 512:W1_OFF + blk * 512 + 512],
                        start=(c == 0), stop=(c == 7)),
                      reads=(lttag, f"w_w1_{blk // 16}"), writes=(btag,))
                E("dve", lambda e, n=n, bank=bank: e.tensor_copy(
                    hTb[:, n * 512:(n + 1) * 512], bank[:, :]),
                  reads=(btag,), writes=tuple(f"{hTt}_{4 * n + k}" for k in range(4)))
                if n >= 1:
                    for m in range(4 * (n - 1), 4 * n):
                        trelu(m)
            for m in range(28, 32):
                trelu(m)

            if nxt is not None:
                ln3_B(*nxt)

            for half, bank, btag in ((0, psO, "psO"), (1, psX, "psX")):
                for m in range(MFF):
                    E("pe", lambda e, m=m, half=half, bank=bank: e.matmul(
                        bank[:, :], hTb[:, m * 128:(m + 1) * 128],
                        warena[:, W2_OFF + m * 1024 + half * 512:
                               W2_OFF + m * 1024 + half * 512 + 512],
                        start=(m == 0), stop=False),
                      reads=(f"{hTt}_{m}", f"w_w2_{m // 8}"), writes=(btag,))
                E("pe", lambda e, half=half, bank=bank: e.matmul(
                    bank[:, :], onesr[:, :], b2r[:, half * 512:half * 512 + 512],
                    start=False, stop=True),
                  reads=("onesr", "b2r"), writes=(btag,))
                E("dve", lambda e, half=half, bank=bank: e.tensor_copy(
                    ff[:, half * 512:half * 512 + 512], bank[:, :]),
                  reads=(btag,), writes=("ff",))

            ln_stats("ff", ff[:], setB)
            ln_norm("ff", ff[:], "lnout", lnout[:], setB)
            E("dve", lambda e, otb=otb, x1b=x1b: e.tensor_add(otb[:], lnout[:], x1b[:]),
              reads=("lnout", x1tag), writes=(otag,))
            E("sync", lambda e, s=s, t=t, otb=otb: e.dma_start(
                out=y_d[s, t * 128:(t + 1) * 128, :], in_=otb[:]),
              reads=(otag,), writes=(f"y_{s}_{t}",), dma=True,
              dsem=f"d_yw{glob % 2}")

        tilesA = [(s, t) for s in range(seqs) for t in range(nt)]
        tilesB = list(tilesA)

        def load_A(idx):
            s, t = tilesA[idx]
            b = idx % 2
            E("sync", lambda e, s=s, t=t, b=b: e.dma_start(
                out=xt[b][:], in_=x_d[s, t * 128:(t + 1) * 128, :]),
              reads=(), writes=(f"xt{b}",), dma=True, dsem=f"d_xl{b}")

        def load_B(idx):
            s, t = tilesB[idx]
            b = idx % 2
            E("sync", lambda e, s=s, t=t, b=b: e.dma_start(
                out=x1t[b][:], in_=x1_d[s, t * 128:(t + 1) * 128, :]),
              reads=(f"x1s_{s}_{t}",), writes=(f"x1t{b}",), dma=True,
              dsem=f"d_x1l{b}")

        load_A(0)
        if len(tilesA) > 1:
            load_A(1)
        if STAGE >= 1 and STAGE not in (45, 46, 47) and STAGE != 0:
            s0_, t0_ = tilesA[0]
            ln1_A(s0_, t0_, 0)
            for chunk in kvq_A(s0_, t0_, 0):
                chunk()
        for i, (s, t) in enumerate(tilesA):
            nxt = (tilesA[i + 1][0], tilesA[i + 1][1], i + 1) if i + 1 < len(tilesA) else None
            tile_A(s, t, i, nxt)
            if i + 2 < len(tilesA):
                load_A(i + 2)
        w2overlap = [("w_kv", "w_q"), ("w_q", "w_proj"), ("w_proj",), ()]
        for i in range(4):
            E("pool", lambda e, i=i: e.dma_start(
                out=warena[:, W2_OFF + i * 8192:W2_OFF + (i + 1) * 8192],
                in_=w2_d[:, i * 8192:(i + 1) * 8192]),
              reads=(), writes=(f"w_w2_{i}",) + w2overlap[i], dma=True,
              dsem="d_w2", dpolicy="all")
        load_B(0)
        if len(tilesB) > 1:
            load_B(1)
        for i, (s, t) in enumerate(tilesB):
            nxt = (tilesB[i + 1][0], tilesB[i + 1][1], i + 1) if i + 1 < len(tilesB) else None
            tile_B(s, t, i, nxt)
            if i + 2 < len(tilesB):
                load_B(i + 2)

        items = tl.finalize()
        totals = dict(tl.counters)
        eng_dsems = {}
        for op in tl.ops:
            if op["dma"]:
                eng_dsems.setdefault(op["eng"], set()).add(op["dsem"])

        def replay(eng):
            def run(e):
                for it in items[eng]:
                    if it[0] == "wait":
                        e.wait_ge(sems[it[1]], it[2])
                    elif it[0] == "drain":
                        e.drain()
                    else:
                        op = it[1]
                        inst = op["fn"](e)
                        if op.get("post_drain") and op["ev"] is not None:
                            e.drain().then_inc(sems[op["ev"][0]], 1)
                        elif op["ev"] is not None:
                            step = 16 if op["dma"] else 1
                            inst.then_inc(sems[op["ev"][0]], step)
                for dsem in sorted(eng_dsems.get(eng, ())):
                    e.wait_ge(sems[dsem], totals[dsem])
            return run

        with nc.Block() as block:
            block.sync(replay("sync"))
            block.tensor(replay("pe"))
            block.scalar(replay("act"))
            block.vector(replay("dve"))
            block.gpsimd(replay("pool"))

    return nc



def _host_prep(Wq, Wkv, Wproj, w1, b1, w2, b2, nt=NT):
    f32 = np.float32
    wqT = np.ascontiguousarray(Wq.T).astype(f32, copy=False)
    wqT = wqT.reshape(D, H, HD)[:, PERM, :].reshape(D, D)
    wq_l = np.ascontiguousarray(wqT.reshape(8, 128, D).transpose(1, 0, 2).reshape(128, 8192))
    wkvT = np.ascontiguousarray(Wkv.T).astype(f32, copy=False)
    wkv_l = np.ascontiguousarray(wkvT.reshape(8, 128, 512).transpose(1, 0, 2).reshape(128, 4096))
    wpT = np.ascontiguousarray(Wproj.T).astype(f32, copy=False)
    wpT = wpT.reshape(H, HD, D)[PERM, :, :].reshape(D, D)
    wp_l = np.ascontiguousarray(wpT.reshape(8, 128, D).transpose(1, 0, 2).reshape(128, 8192))
    w1_l = np.ascontiguousarray(
        w1.reshape(8, 128, 8, 512).transpose(1, 0, 2, 3).reshape(128, 32768))
    w2_l = np.ascontiguousarray(
        w2.reshape(MFF, 128, D).transpose(1, 0, 2).reshape(128, 32768))
    b1_l = np.ascontiguousarray(b1.reshape(MFF, 128).T)
    b2_l = np.ascontiguousarray(b2[None, :])

    theta = (1.0 / 10000.0 ** (np.arange(0, HD, 2, dtype=f32) / HD)).astype(f32)
    pos = np.arange(nt * 128, dtype=f32)
    freqs = np.outer(pos, theta).astype(f32)
    cosv = np.cos(freqs)
    sinv = np.sin(freqs)
    cos_full = np.concatenate([cosv, cosv], axis=1)
    sin_s = np.concatenate([-sinv, sinv], axis=1)
    cos_l = np.ascontiguousarray(
        cos_full.reshape(nt, 128, HD).transpose(1, 0, 2).reshape(128, nt * HD)).astype(f32)
    sin_l = np.ascontiguousarray(
        sin_s.reshape(nt, 128, HD).transpose(1, 0, 2).reshape(128, nt * HD)).astype(f32)

    ident = np.eye(128, dtype=ml_dtypes.bfloat16)
    p = np.arange(128)
    m = np.arange(128)
    sel0 = (m[None, :] == (p % 64)[:, None]).astype(ml_dtypes.bfloat16)
    sel1 = (m[None, :] == 64 + (p % 64)[:, None]).astype(ml_dtypes.bfloat16)
    BIG = np.float32(-65536.0)
    mself = np.where(np.arange(128)[None, :] > np.arange(128)[:, None], BIG, 0.0)
    mold = np.where(np.arange(128)[None, :] <= np.arange(128)[:, None], BIG, 0.0)
    msk0s = mself[(p % 64), :].astype(ml_dtypes.bfloat16)
    msk1s = mself[64 + (p % 64), :].astype(ml_dtypes.bfloat16)
    msk0o = mold[(p % 64), :].astype(ml_dtypes.bfloat16)
    msk1o = mold[64 + (p % 64), :].astype(ml_dtypes.bfloat16)
    selmsk = np.ascontiguousarray(
        np.concatenate([sel0, sel1, msk0s, msk1s, msk0o, msk1o], axis=1))

    return dict(wq=wq_l, wkv=wkv_l, wp=wp_l, w1=w1_l, w2=w2_l, b1f=b1_l, b2bc=b2_l,
                cost=cos_l, sint=sin_l, ident=ident, selmsk=selmsk)


def kernel(L0_attn_x, L0_attn_Wq, L0_attn_Wkv, L0_attn_Wproj,
           L0_ffn_w1, L0_ffn_b1, L0_ffn_w2, L0_ffn_b2):
    x = np.ascontiguousarray(np.asarray(L0_attn_x, dtype=np.float32))
    B = x.shape[0]
    common = _host_prep(np.asarray(L0_attn_Wq, np.float32),
                        np.asarray(L0_attn_Wkv, np.float32),
                        np.asarray(L0_attn_Wproj, np.float32),
                        np.asarray(L0_ffn_w1, np.float32),
                        np.asarray(L0_ffn_b1, np.float32),
                        np.asarray(L0_ffn_w2, np.float32),
                        np.asarray(L0_ffn_b2, np.float32))

    nc = build_nc()
    in_maps = []
    for c in range(NCORES):
        m = dict(common)
        m["x"] = np.ascontiguousarray(x[c * SEQS:(c + 1) * SEQS])
        in_maps.append(m)
    res = run_bass_kernel_spmd(nc, in_maps, core_ids=list(range(NCORES)))
    out = np.concatenate([r["y"] for r in res.results], axis=0)
    return out


# version 12
# speedup vs baseline: 1.0039x; 1.0039x over previous
import contextlib
import numpy as np
import ml_dtypes

import concourse.bass as bass
import concourse.mybir as mybir
from concourse.bass_utils import run_bass_kernel_spmd

F32 = mybir.dt.float32
BF16 = mybir.dt.bfloat16
AF = mybir.ActivationFunctionType
OP = mybir.AluOpType
AX = mybir.AxisListType

NCORES = 8
SEQS = 2
T = 2048
D = 1024
NT = T // 128
H = 16
KVH = 4
HD = 64
WIN = 512
DFF = 4096
MFF = DFF // 128
LNEPS = 1e-5
QKEPS = 1e-6
STAGE = 5
INTERLEAVE = True
NOPAIRS = False
NOKVQ = False
SUB = 99
QKMODE = 'full'

PERM = [0, 4, 1, 5, 2, 6, 3, 7, 8, 12, 9, 13, 10, 14, 11, 15]
GOF = [PERM[s] // 4 for s in range(H)]

W1_OFF = 0
W2_OFF = 32768
WKV_OFF = W2_OFF
WQ_OFF = WKV_OFF + 4096
WP_OFF = WQ_OFF + 8192
WARENA_N = 65536

ENGINES = ("pe", "act", "dve", "pool", "sync")
DMA_SEM = {"sync": "d_sync", "pool": "d_pool"}
CMP_SEM = {"pe": "c_pe", "act": "c_act", "dve": "c_dve", "pool": "c_pool"}


class TL:

    def __init__(self):
        self.ops = []

    def emit(self, eng, fn, reads=(), writes=(), dma=False, post_drain=False,
             dsem=None, dpolicy="slot"):
        op = {
            "eng": eng, "fn": fn, "reads": tuple(reads), "writes": tuple(writes),
            "dma": dma, "idx": len(self.ops), "ev": None, "need_ev": dma,
            "post_drain": post_drain, "dsem": dsem, "dpolicy": dpolicy,
        }
        self.ops.append(op)
        return op

    def finalize(self):
        last_w = {}
        last_r = {}
        for op in self.ops:
            deps = {}
            for t in op["reads"]:
                w = last_w.get(t)
                if w is not None:
                    deps[w["idx"]] = w
            for t in op["writes"]:
                w = last_w.get(t)
                if w is not None and not (w["dma"] and op["dma"]):
                    deps[w["idx"]] = w
                for r in last_r.get(t, {}).values():
                    deps[r["idx"]] = r
            op["deps"] = [self.ops[i] for i in sorted(deps)]
            for t in op["reads"]:
                last_r.setdefault(t, {})[op["eng"]] = op
            for t in op["writes"]:
                last_w[t] = op
                last_r[t] = {}
        for op in self.ops:
            for d in op["deps"]:
                if d["dma"] or d["eng"] != op["eng"] or op["eng"] != "pe":
                    d["need_ev"] = True
        counters = {}
        for op in self.ops:
            if op["dma"]:
                sem = op["dsem"]
                counters[sem] = counters.get(sem, 0) + 16
                op["ev"] = (sem, counters[sem])
            elif op["need_ev"]:
                sem = CMP_SEM[op["eng"]]
                counters[sem] = counters.get(sem, 0) + 1
                op["ev"] = (sem, counters[sem])
        for op in self.ops:
            if op["dma"] and op["dpolicy"] == "all":
                op["ev"] = (op["dsem"], counters[op["dsem"]])
        items = {e: [] for e in ENGINES}
        watermark = {}
        last_drain = {e: -1 for e in ENGINES}
        for op in self.ops:
            eng = op["eng"]
            waits = {}
            need_drain = False
            for d in op["deps"]:
                if d["eng"] == "pe" and eng == "pe" and not d["dma"] and not op["dma"]:
                    continue
                if d["eng"] == eng and not d["dma"]:
                    if d["idx"] > last_drain[eng]:
                        need_drain = True
                    continue
                sem, val = d["ev"]
                if waits.get(sem, 0) < val:
                    waits[sem] = val
            if need_drain:
                items[eng].append(("drain",))
                last_drain[eng] = op["idx"] - 1
            for sem, val in sorted(waits.items()):
                if watermark.get((eng, sem), 0) < val:
                    items[eng].append(("wait", sem, val))
                    watermark[(eng, sem)] = val
            items[eng].append(("op", op))
        self.counters = counters
        return items


def ceil_div(a, b):
    return -(-a // b)


def build_nc(nt=NT, seqs=SEQS):
    nc = bass.Bass("TRN2", target_bir_lowering=False)
    Tq = nt * 128

    x_d = nc.dram_tensor("x", [seqs, Tq, D], F32, kind="ExternalInput").ap()
    wq_d = nc.dram_tensor("wq", [128, 8192], F32, kind="ExternalInput").ap()
    wkv_d = nc.dram_tensor("wkv", [128, 4096], F32, kind="ExternalInput").ap()
    wp_d = nc.dram_tensor("wp", [128, 8192], F32, kind="ExternalInput").ap()
    w1_d = nc.dram_tensor("w1", [128, 32768], F32, kind="ExternalInput").ap()
    w2_d = nc.dram_tensor("w2", [128, 32768], F32, kind="ExternalInput").ap()
    b1_d = nc.dram_tensor("b1f", [128, MFF], F32, kind="ExternalInput").ap()
    b2_d = nc.dram_tensor("b2bc", [1, D], F32, kind="ExternalInput").ap()
    cos_d = nc.dram_tensor("cost", [128, nt * HD], F32, kind="ExternalInput").ap()
    sin_d = nc.dram_tensor("sint", [128, nt * HD], F32, kind="ExternalInput").ap()
    ident_d = nc.dram_tensor("ident", [128, 128], BF16, kind="ExternalInput").ap()
    selmsk_d = nc.dram_tensor("selmsk", [128, 6 * 128], BF16, kind="ExternalInput").ap()
    x1_d = nc.dram_tensor("x1s", [seqs, Tq, D], F32, kind="Internal").ap()
    y_d = nc.dram_tensor("y", [seqs, Tq, D], F32, kind="ExternalOutput").ap()

    tl = TL()

    with contextlib.ExitStack() as ctx:
        sb = lambda name, shape, dt: ctx.enter_context(nc.sbuf_tensor("sb_" + name, shape, dt))
        ps = lambda name, shape, dt: ctx.enter_context(nc.psum_tensor(name, shape, dt))

        warena = sb("warena", [128, WARENA_N], BF16)
        cost = sb("cost", [128, nt * HD], BF16)
        sint = sb("sint", [128, nt * HD], BF16)
        b1f = sb("b1f", [128, MFF], F32)
        ident = sb("ident", [128, 128], BF16)
        selmsk = sb("selmsk", [128, 6 * 128], BF16)
        xt = [sb("xt0", [128, D], F32), sb("xt1", [128, D], F32)]
        x1t = [sb("x1t0", [128, D], F32), sb("x1t1", [128, D], F32)]
        junk = sb("junk", [128, D], BF16)
        sqr = sb("sqr", [128, D], BF16)
        lnout = sb("lnout", [128, D], F32)
        ao = sb("ao", [128, D], F32)
        ln1 = sb("ln1", [128, D], BF16)
        lt = [sb("lt0", [128, D], BF16), sb("lt1", [128, D], BF16)]
        qf = sb("qf", [128, D], BF16)
        qr = sb("qr", [128, D], BF16)
        qT = [sb("qT0", [128, D], BF16), sb("qT1", [128, D], BF16)]
        kf = sb("kf", [128, 256], BF16)
        kr4 = sb("kr4", [128, 256], BF16)
        krb = sb("krb", [128, 256], BF16)
        KT = sb("KT", [128, 6 * 256], BF16)
        VR = sb("VR", [128, 6 * 256], BF16)
        _a0 = [sb("aT00", [128, 640], BF16), sb("aT01", [128, 640], BF16)]
        Eb = [[sb("E00", [128, 640], BF16), sb("E01", [128, 640], BF16)],
              [sb("E10", [128, 640], BF16), sb("E11", [128, 640], BF16)]]
        aT = [_a0, _a0]
        yT = sb("yT", [128, D], BF16)
        _h0 = sb("hT0", [128, DFF], BF16)
        hT = [_h0, _h0]
        ff = sb("ff", [128, D], F32)
        st = sb("st", [128, 128], F32)
        b2r = sb("b2r", [1, D], BF16)
        onesr = sb("onesr", [1, 128], BF16)
        epsc = sb("epsc", [128, 1], F32)

        pjq0 = ps("pjq0", [128, 512], F32)
        pjq1 = ps("pjq1", [128, 512], F32)
        pjk = ps("pjk", [128, 512], F32)
        psE = ps("psE", [128, 512], F32)
        psO = ps("psO", [128, 512], F32)
        psAV = ps("psAV", [128, 512], F32)
        psX = ps("psX", [128, 512], F32)
        ptr = ps("ptr", [128, 1024], BF16)

        sems = {}
        for s in ("c_pe", "c_act", "c_dve", "c_pool", "d_wc", "d_w1", "d_w2",
                  "d_xl0", "d_xl1", "d_x1l0", "d_x1l1", "d_x1w0", "d_x1w1",
                  "d_yw0", "d_yw1"):
            sems[s] = ctx.enter_context(nc.semaphore(s))

        E = tl.emit

        SETBASE = {"A0": 0, "A1": 8, "B0": 16, "B1": 24}
        LNC = {"s1": 0, "s2": 1, "nm": 2, "t2": 3, "lnv": 4, "rstd": 5}

        def lnc(set_, name):
            return st[:, SETBASE[set_] + LNC[name]:SETBASE[set_] + LNC[name] + 1]

        def qkc(p, name, w=1):
            base = 32 + 40 * p + {"qss": 0, "qn": 16, "kss": 32, "kn": 36}[name]
            return st[:, base:base + w]

        def dnc(cp, i, name, w=1):
            base = 112 + cp * 8 + {"da": 0, "db": 2, "den": 4, "rc": 6,
                                   "lnb": 6}[name] + i
            return st[:, base:base + w]

        E("dve", lambda e: e.memset(epsc[:], LNEPS), writes=("epsc",))
        E("dve", lambda e: e.memset(onesr[:], 1.0), writes=("onesr",))
        E("pool", lambda e: e.dma_start(out=ident[:], in_=ident_d), writes=("ident",), dma=True, dsem="d_wc", dpolicy="all")
        E("pool", lambda e: e.dma_start(out=selmsk[:], in_=selmsk_d), writes=("selmsk",), dma=True, dsem="d_wc", dpolicy="all")
        E("pool", lambda e: e.dma_start(out=cost[:], in_=cos_d), writes=("cost",), dma=True, dsem="d_wc", dpolicy="all")
        E("pool", lambda e: e.dma_start(out=sint[:], in_=sin_d), writes=("sint",), dma=True, dsem="d_wc", dpolicy="all")
        E("pool", lambda e: e.dma_start(out=b1f[:], in_=b1_d), writes=("b1f",), dma=True, dsem="d_wc", dpolicy="all")
        E("pool", lambda e: e.dma_start(out=b2r[:], in_=b2_d), writes=("b2r",), dma=True, dsem="d_wc", dpolicy="all")
        E("pool", lambda e: e.dma_start(out=warena[:, WKV_OFF:WKV_OFF + 4096], in_=wkv_d),
          writes=("w_kv", "w_w2_0"), dma=True, dsem="d_wc", dpolicy="all")
        E("pool", lambda e: e.dma_start(out=warena[:, WQ_OFF:WQ_OFF + 8192], in_=wq_d),
          writes=("w_q", "w_w2_0", "w_w2_1"), dma=True, dsem="d_wc", dpolicy="all")
        E("pool", lambda e: e.dma_start(out=warena[:, WP_OFF:WP_OFF + 8192], in_=wp_d),
          writes=("w_proj", "w_w2_1", "w_w2_2"), dma=True, dsem="d_wc", dpolicy="all")
        for i in range(4):
            E("pool", lambda e, i=i: e.dma_start(
                out=warena[:, W1_OFF + i * 8192:W1_OFF + (i + 1) * 8192],
                in_=w1_d[:, i * 8192:(i + 1) * 8192]), writes=(f"w_w1_{i}",),
              dma=True, dsem="d_w1", dpolicy="all")

        def ln_stats(src_tag, src_ap, set_):
            tg = lambda n: f"st_{set_}_{n}"
            E("act", lambda e: e.activation(junk[:], src_ap, AF.Square, scale=1.0 / 32.0,
                                            accum_out=lnc(set_, "s2")),
              reads=(src_tag,), writes=(tg("s2"), "jkJ", "jkS"))
            E("dve", lambda e: e.tensor_reduce(lnc(set_, "s1"), src_ap, AX.X, OP.add),
              reads=(src_tag,), writes=(tg("s1"),))
            E("dve", lambda e: e.tensor_scalar(lnc(set_, "nm"), lnc(set_, "s1"),
                                               -1.0 / D, None, OP.mult),
              reads=(tg("s1"),), writes=(tg("nm"),))
            E("dve", lambda e: e.scalar_tensor_tensor(
                lnc(set_, "t2"), lnc(set_, "nm"), lnc(set_, "nm"), lnc(set_, "s2"),
                OP.mult, OP.subtract),
              reads=(tg("nm"), tg("s2")), writes=(tg("t2"),))
            E("act", lambda e: e.activation(lnc(set_, "lnv"), lnc(set_, "t2"), AF.Ln,
                                            scale=-1.0, bias=epsc[:]),
              reads=(tg("t2"), "epsc"), writes=(tg("lnv"),))
            E("act", lambda e: e.activation(lnc(set_, "rstd"), lnc(set_, "lnv"),
                                            AF.Exp, scale=-0.5),
              reads=(tg("lnv"),), writes=(tg("rstd"),))

        def ln_norm(src_tag, src_ap, out_tag, out_ap, set_):
            E("dve", lambda e: e.tensor_scalar(out_ap, src_ap, lnc(set_, "nm"),
                                               lnc(set_, "rstd"), OP.add, OP.mult),
              reads=(src_tag, f"st_{set_}_nm", f"st_{set_}_rstd"), writes=(out_tag,))

        def transposes(src_tag, src_ap_fn, nchunks, dst_tag, dst_sb):
            for c in range(nchunks):
                E("pe", lambda e, c=c: e.transpose(
                    ptr[:, c * 128:(c + 1) * 128], src_ap_fn(c), ident[:]),
                  reads=(src_tag, "ident"), writes=("ptr",))
            off = 0
            while off < nchunks * 128:
                w = min(512, nchunks * 128 - off)
                E("dve", lambda e, off=off, w=w: e.tensor_copy(
                    dst_sb[:, off:off + w], ptr[:, off:off + w]),
                  reads=("ptr",), writes=(dst_tag,))
                off += w

        def ln1_A(s, t, glob):
            p = glob % 2
            ln_stats(f"xt{p}", xt[p][:], f"A{p}")
            ln_norm(f"xt{p}", xt[p][:], "ln1", ln1[:], f"A{p}")
            transposes("ln1", lambda c: ln1[:, c * 128:(c + 1) * 128], 8,
                       f"lt{p}", lt[p])

        def kvq_A(s, t, glob):
            p = glob % 2
            ltb, lttag = lt[p], f"lt{p}"
            qTb, qTtag = qT[p], f"qT{p}"
            slot = glob % 6
            cslice = lambda j0, w: cost[:, t * HD + j0: t * HD + j0 + w]
            sslice = lambda j0, w: sint[:, t * HD + j0: t * HD + j0 + w]

            def kv_mms():
                for c in range(8):
                    E("pe", lambda e, c=c: e.matmul(
                        pjq1[:, :], ltb[:, c * 128:(c + 1) * 128],
                        warena[:, WKV_OFF + c * 512:WKV_OFF + (c + 1) * 512],
                        start=(c == 0), stop=(c == 7)),
                      reads=(lttag, "w_kv"), writes=("pjq1",))

            def k_chain():
                E("act", lambda e: e.activation(sqr[:, 0:256], pjq1[:, 0:256], AF.Square),
                  reads=("pjq1",), writes=("sqr",))
                E("dve", lambda e: e.tensor_reduce(
                    qkc(p, "kss", 4), sqr[:, 0:256].rearrange("p (g w) -> p g w", g=4),
                    AX.X, OP.add),
                  reads=("sqr",), writes=(f"st_kss{p}",))
                E("act", lambda e: e.activation(qkc(p, "kss", 4), qkc(p, "kss", 4), AF.Ln),
                  reads=(f"st_kss{p}",), writes=(f"st_kss{p}",))
                E("act", lambda e: e.activation(qkc(p, "kss", 4), qkc(p, "kss", 4),
                                                AF.Exp, scale=0.5),
                  reads=(f"st_kss{p}",), writes=(f"st_kss{p}",))
                E("dve", lambda e: e.tensor_scalar(qkc(p, "kss", 4), qkc(p, "kss", 4),
                                                   QKEPS, None, OP.add),
                  reads=(f"st_kss{p}",), writes=(f"st_kss{p}",))
                E("dve", lambda e: e.reciprocal(qkc(p, "kss", 4), qkc(p, "kss", 4)),
                  reads=(f"st_kss{p}",), writes=(f"st_kss{p}",))
                E("dve", lambda e: e.tensor_tensor(
                    kf[:].rearrange("p (g w) -> p g w", g=4),
                    pjq1[:, 0:256].rearrange("p (g w) -> p g w", g=4),
                    qkc(p, "kss", 4).unsqueeze(2).broadcast_to((128, 4, HD)), OP.mult),
                  reads=("pjq1", f"st_kss{p}"), writes=("kf",))
                E("dve", lambda e: e.tensor_copy(
                    VR[:, slot * 256:slot * 256 + 256], pjq1[:, 256:512]),
                  reads=("pjq1",), writes=(f"V{slot}",))
                E("dve", lambda e: e.tensor_tensor(
                    kr4[:].rearrange("p (g w) -> p g w", g=4)[:, :, 0:32],
                    kf[:].rearrange("p (g w) -> p g w", g=4)[:, :, 32:64],
                    sslice(0, 32).unsqueeze(1).broadcast_to((128, 4, 32)), OP.mult),
                  reads=("kf", "sint"), writes=("kr4",))
                E("dve", lambda e: e.tensor_tensor(
                    kr4[:].rearrange("p (g w) -> p g w", g=4)[:, :, 32:64],
                    kf[:].rearrange("p (g w) -> p g w", g=4)[:, :, 0:32],
                    sslice(32, 32).unsqueeze(1).broadcast_to((128, 4, 32)), OP.mult),
                  reads=("kf", "sint"), writes=("kr4",))
                E("dve", lambda e: e.tensor_tensor(
                    krb[:].rearrange("p (g w) -> p g w", g=4),
                    kf[:].rearrange("p (g w) -> p g w", g=4),
                    cslice(0, HD).unsqueeze(1).broadcast_to((128, 4, HD)), OP.mult),
                  reads=("kf", "cost"), writes=("krb",))
                E("dve", lambda e: e.tensor_add(krb[:], krb[:], kr4[:]),
                  reads=("krb", "kr4"), writes=("krb",))

            def kT_tr():
                for c in range(2):
                    E("pe", lambda e, c=c: e.transpose(
                        ptr[:, c * 128:(c + 1) * 128], krb[:, c * 128:(c + 1) * 128],
                        ident[:]),
                      reads=("krb", "ident"), writes=("ptr",))
                E("dve", lambda e: e.tensor_copy(
                    KT[:, slot * 256:slot * 256 + 256], ptr[:, 0:256]),
                  reads=("ptr",), writes=(f"K{slot}",))

            def q_half(half):
                import os
                q0mode = os.environ.get("Q0MODE", "full")
                for c in range(8):
                    E("pe", lambda e, c=c, half=half: e.matmul(
                        pjq0[:, :], ltb[:, c * 128:(c + 1) * 128],
                        warena[:, WQ_OFF + c * 1024 + half * 512:
                               WQ_OFF + c * 1024 + half * 512 + 512],
                        start=(c == 0), stop=(c == 7)),
                      reads=(lttag, "w_q"), writes=("pjq0",))
                if q0mode == "mmonly":
                    return
                E("act", lambda e, half=half: e.activation(
                    sqr[:, half * 512:half * 512 + 512], pjq0[:, :], AF.Square),
                  reads=("pjq0",), writes=("sqr", "q0tok"))
                if q0mode == "nocopy":
                    return
                E("dve", lambda e, half=half: e.tensor_copy(
                    qf[:, half * 512:half * 512 + 512], pjq0[:, :]),
                  reads=("pjq0", "q0tok"), writes=("qf",))

            def q_chain():
                E("dve", lambda e: e.tensor_reduce(
                    qkc(p, "qn", 16), sqr[:].rearrange("p (g w) -> p g w", g=16),
                    AX.X, OP.add),
                  reads=("sqr",), writes=(f"st_qn{p}",))
                E("act", lambda e: e.activation(qkc(p, "qn", 16), qkc(p, "qn", 16), AF.Ln),
                  reads=(f"st_qn{p}",), writes=(f"st_qn{p}",))
                E("act", lambda e: e.activation(qkc(p, "qn", 16), qkc(p, "qn", 16),
                                                AF.Exp, scale=0.5),
                  reads=(f"st_qn{p}",), writes=(f"st_qn{p}",))
                E("dve", lambda e: e.tensor_scalar(qkc(p, "qn", 16), qkc(p, "qn", 16),
                                                   QKEPS, None, OP.add),
                  reads=(f"st_qn{p}",), writes=(f"st_qn{p}",))
                E("dve", lambda e: e.reciprocal(qkc(p, "qn", 16), qkc(p, "qn", 16)),
                  reads=(f"st_qn{p}",), writes=(f"st_qn{p}",))
                E("dve", lambda e: e.tensor_tensor(
                    qr[:].rearrange("p (g w) -> p g w", g=16),
                    qf[:].rearrange("p (g w) -> p g w", g=16),
                    qkc(p, "qn", 16).unsqueeze(2).broadcast_to((128, 16, HD)), OP.mult),
                  reads=("qf", f"st_qn{p}"), writes=("qr",))
                E("dve", lambda e: e.tensor_tensor(
                    sqr[:].rearrange("p (g w) -> p g w", g=16)[:, :, 0:32],
                    qr[:].rearrange("p (g w) -> p g w", g=16)[:, :, 32:64],
                    sslice(0, 32).unsqueeze(1).broadcast_to((128, 16, 32)), OP.mult),
                  reads=("qr", "sint"), writes=("sqr",))
                E("dve", lambda e: e.tensor_tensor(
                    sqr[:].rearrange("p (g w) -> p g w", g=16)[:, :, 32:64],
                    qr[:].rearrange("p (g w) -> p g w", g=16)[:, :, 0:32],
                    sslice(32, 32).unsqueeze(1).broadcast_to((128, 16, 32)), OP.mult),
                  reads=("qr", "sint"), writes=("sqr",))
                E("dve", lambda e: e.tensor_tensor(
                    qf[:].rearrange("p (g w) -> p g w", g=16),
                    qr[:].rearrange("p (g w) -> p g w", g=16),
                    cslice(0, HD).unsqueeze(1).broadcast_to((128, 16, HD)), OP.mult),
                  reads=("qr", "cost"), writes=("qf",))
                E("dve", lambda e: e.tensor_add(qr[:], qf[:], sqr[:]),
                  reads=("qf", "sqr"), writes=("qr",))

            def qT_tr():
                transposes("qr", lambda c: qr[:, c * 128:(c + 1) * 128], 8, qTtag, qTb)

            import os
            mask = int(os.environ.get("KVQMASK", "127"))
            chunks = [kv_mms, k_chain, kT_tr, lambda: q_half(0), lambda: q_half(1),
                      q_chain, qT_tr]
            return [c for i, c in enumerate(chunks) if mask & (1 << i)]

        def tile_A(s, t, glob, nxt=None):
            p = glob % 2
            xtb, xtag = xt[p], f"xt{p}"
            x1b, x1tag = x1t[p], f"x1t{p}"
            qTb, qTtag = qT[p], f"qT{p}"
            setB = f"B{p}"

            def fallback_store():
                E("dve", lambda e: e.tensor_copy(x1b[:], xtb[:]),
                  reads=(xtag,), writes=(x1tag,))
                E("sync", lambda e: e.dma_start(
                    out=x1_d[s, t * 128:(t + 1) * 128, :], in_=x1b[:]),
                  reads=(x1tag,), writes=(f"x1s_{s}_{t}",), dma=True,
                  dsem=f"d_x1w{glob % 2}")

            if STAGE == 0 or STAGE in (45, 46, 47):
                fallback_store()
                return

            if nxt is not None:
                ln1_A(*nxt)
            kvq = kvq_A(*nxt) if nxt is not None else []

            j0 = max(0, t - 4)
            J = list(range(j0, t))
            na = len(J)
            mask_old = (t >= 4)

            def pair_phase1(c):
                cp = c % 2
                s0, s1_ = 2 * c, 2 * c + 1
                heads = []
                for i, (sl, bank, btag, base) in enumerate(
                        ((s0, psE, "psE", 0), (s1_, psO, "psO", 64))):
                    sbank, sbtag = (pjk, "pjk") if i == 0 else (psX, "psX")
                    heads.append(dict(
                        i=i, sl=sl, bank=bank, btag=btag, base=base, g=GOF[sl],
                        sbank=sbank, sbtag=sbtag, soff=cp * 128,
                        qap=qTb[base:base + 64, (sl // 2) * 128:(sl // 2) * 128 + 128],
                        eb=Eb[cp][i], etag=f"E{cp}{i}"))

                def kslice(hd, j):
                    sl6 = (glob - (t - j)) % 6
                    return KT[hd["base"]:hd["base"] + 64,
                              sl6 * 256 + (hd["g"] // 2) * 128:
                              sl6 * 256 + (hd["g"] // 2) * 128 + 128]

                for jj, j in enumerate(J):
                    for hd in heads:
                        bank, btag, qap = hd["bank"], hd["btag"], hd["qap"]
                        kap = kslice(hd, j)
                        if jj == 0 and mask_old:
                            E("pe", lambda e, bank=bank, qap=qap, kap=kap, jj=jj: e.matmul(
                                bank[:, jj * 128:(jj + 1) * 128], qap, kap,
                                start=True, stop=False),
                              reads=(qTtag, f"K{(glob - (t - j)) % 6}"), writes=(btag,))
                            for r in range(2):
                                selap = selmsk[hd["base"]:hd["base"] + 64,
                                               r * 128:(r + 1) * 128]
                                mskap = selmsk[hd["base"]:hd["base"] + 64,
                                               (4 + r) * 128:(5 + r) * 128]
                                E("pe", lambda e, bank=bank, selap=selap, mskap=mskap,
                                  jj=jj, r=r: e.matmul(
                                      bank[:, jj * 128:(jj + 1) * 128], selap, mskap,
                                      start=False, stop=(r == 1)),
                                  reads=("selmsk",), writes=(btag,))
                        else:
                            E("pe", lambda e, bank=bank, qap=qap, kap=kap, jj=jj: e.matmul(
                                bank[:, jj * 128:(jj + 1) * 128], qap, kap,
                                start=True, stop=True),
                              reads=(qTtag, f"K{(glob - (t - j)) % 6}"), writes=(btag,))
                for hd in heads:
                    sbank, sbtag, qap = hd["sbank"], hd["sbtag"], hd["qap"]
                    so = hd["soff"]
                    kap = kslice(hd, t)
                    E("pe", lambda e, sbank=sbank, qap=qap, kap=kap, so=so: e.matmul(
                        sbank[:, so:so + 128], qap, kap, start=True, stop=False),
                      reads=(qTtag, f"K{glob % 6}"), writes=(sbtag,))
                    for r in range(2):
                        selap = selmsk[hd["base"]:hd["base"] + 64, r * 128:(r + 1) * 128]
                        mskap = selmsk[hd["base"]:hd["base"] + 64,
                                       (2 + r) * 128:(3 + r) * 128]
                        E("pe", lambda e, sbank=sbank, selap=selap, mskap=mskap, r=r,
                          so=so: e.matmul(sbank[:, so:so + 128], selap, mskap,
                                          start=False, stop=(r == 1)),
                          reads=("selmsk",), writes=(sbtag,))
                for hd in heads:
                    i = hd["i"]
                    if na > 0:
                        E("act", lambda e, hd=hd, cp=cp: e.activation(
                            junk[:, 0:na * 128], hd["bank"][:, 0:na * 128],
                            AF.Exp, scale=0.125, accum_out=dnc(cp, hd["i"], "da")),
                          reads=(hd["btag"],), writes=(f"st_da{cp}{i}", "jkJ"))
                    E("act", lambda e, hd=hd, cp=cp: e.activation(
                        junk[:, 512:640], hd["sbank"][:, hd["soff"]:hd["soff"] + 128],
                        AF.Exp, scale=0.125, accum_out=dnc(cp, hd["i"], "db")),
                      reads=(hd["sbtag"],), writes=(f"st_db{cp}{i}", "jkS"))
                if na > 0:
                    E("dve", lambda e, cp=cp: e.tensor_tensor(
                        dnc(cp, 0, "den", 2), dnc(cp, 0, "da", 2),
                        dnc(cp, 0, "db", 2), OP.add),
                      reads=(f"st_da{cp}0", f"st_da{cp}1", f"st_db{cp}0",
                             f"st_db{cp}1"),
                      writes=(f"st_den{cp}0", f"st_den{cp}1"))
                else:
                    E("dve", lambda e, cp=cp: e.tensor_copy(
                        dnc(cp, 0, "den", 2), dnc(cp, 0, "db", 2)),
                      reads=(f"st_db{cp}0", f"st_db{cp}1"),
                      writes=(f"st_den{cp}0", f"st_den{cp}1"))
                E("dve", lambda e, cp=cp: e.reciprocal(
                    dnc(cp, 0, "rc", 2), dnc(cp, 0, "den", 2)),
                  reads=(f"st_den{cp}0", f"st_den{cp}1"),
                  writes=(f"st_rc{cp}0", f"st_rc{cp}1"))
                E("act", lambda e, cp=cp: e.activation(
                    dnc(cp, 0, "lnb", 2), dnc(cp, 0, "rc", 2), AF.Ln),
                  reads=(f"st_rc{cp}0", f"st_rc{cp}1"),
                  writes=(f"st_rc{cp}0", f"st_rc{cp}1"))
                for hd in heads:
                    i = hd["i"]
                    if na > 0:
                        E("act", lambda e, hd=hd, cp=cp: e.activation(
                            hd["eb"][:, 0:na * 128], hd["bank"][:, 0:na * 128],
                            AF.Exp, scale=0.125, bias=dnc(cp, hd["i"], "lnb")),
                          reads=(hd["btag"], f"st_rc{cp}{i}"), writes=(hd["etag"],))
                    E("act", lambda e, hd=hd, cp=cp: e.activation(
                        hd["eb"][:, na * 128:(na + 1) * 128],
                        hd["sbank"][:, hd["soff"]:hd["soff"] + 128],
                        AF.Exp, scale=0.125, bias=dnc(cp, hd["i"], "lnb")),
                      reads=(hd["sbtag"], f"st_rc{cp}{i}"), writes=(hd["etag"],))
                return heads

            def pair_phase2(c, heads):
                cp = c % 2
                nch = na + 1
                ev_slots = [0, 1, 2, 3, 4][:nch]
                od_slots = [5, 6, 7, 0, 1][:nch]
                for i, slots in ((0, ev_slots), (1, od_slots)):
                    hd = heads[i]
                    eb, etag = hd["eb"], hd["etag"]
                    atb, atag = aT[cp][i], f"aT{i}"
                    for jj, slot in enumerate(slots):
                        E("pe", lambda e, eb=eb, jj=jj, slot=slot: e.transpose(
                            ptr[:, slot * 128:(slot + 1) * 128],
                            eb[:, jj * 128:(jj + 1) * 128], ident[:]),
                          reads=(etag, "ident"), writes=("ptr",))
                    run = []
                    for jj, slot in enumerate(slots + [None]):
                        if run and (slot is None or slot != run[-1][1] + 1):
                            j0r, s0r = run[0]
                            w = len(run) * 128
                            E("dve", lambda e, atb=atb, j0r=j0r, s0r=s0r, w=w:
                              e.tensor_copy(atb[:, j0r * 128:j0r * 128 + w],
                                            ptr[:, s0r * 128:s0r * 128 + w]),
                              reads=("ptr",), writes=(atag,))
                            run = []
                        if slot is not None:
                            run.append((jj, slot))
                s0, s1_ = 2 * c, 2 * c + 1
                for i, sl in enumerate((s0, s1_)):
                    g = GOF[sl]
                    hd = heads[i]
                    atb, atag = aT[cp][i], f"aT{i}"
                    outap = psAV[i * 64:(i + 1) * 64,
                                 cp * 256 + i * 128:cp * 256 + (i + 1) * 128]
                    chunks = J + [t]
                    for jj, j in enumerate(chunks):
                        sl6 = (glob - (t - j)) % 6
                        vap = VR[:, sl6 * 256 + g * 64:sl6 * 256 + g * 64 + 64]
                        E("pe", lambda e, outap=outap, vap=vap, atb=atb, jj=jj, i=i,
                          first=(jj == 0), last=(jj == len(chunks) - 1): e.matmul(
                              outap, vap, atb[:, jj * 128:(jj + 1) * 128],
                              start=first, stop=last, tile_position=(0, i * 64)),
                          reads=(f"V{sl6}", atag), writes=("psAV",))
                E("dve", lambda e, c=c, cp=cp: e.tensor_copy(
                    yT[0:64, c * 128:(c + 1) * 128], psAV[0:64, cp * 256:cp * 256 + 128]),
                  reads=("psAV",), writes=("yT",))
                E("dve", lambda e, c=c, cp=cp: e.tensor_copy(
                    yT[64:128, c * 128:(c + 1) * 128],
                    psAV[64:128, cp * 256 + 128:cp * 256 + 256]),
                  reads=("psAV",), writes=("yT",))

            if NOKVQ:
                kvq = []
            sched = {1: [0], 2: [1], 3: [3], 4: [4], 5: [5], 6: [2]} if INTERLEAVE else {}
            prev = None
            if not NOPAIRS:
                for c in range(8):
                    heads = pair_phase1(c)
                    if prev is not None:
                        pair_phase2(prev[0], prev[1])
                    for k in sched.get(c, []):
                        if k < len(kvq):
                            kvq[k]()
                    prev = (c, heads)
                pair_phase2(prev[0], prev[1])
                if INTERLEAVE:
                    if len(kvq) > 6:
                        kvq[6]()
                else:
                    for chunk in kvq:
                        chunk()
            else:
                for chunk in kvq:
                    chunk()
                E("dve", lambda e: e.memset(yT[:], 0.0), writes=("yT",))

            if STAGE == 3:
                fallback_store()
                return

            for half, bank, btag in ((0, pjq0, "pjq0"), (1, pjq1, "pjq1")):
                for c in range(8):
                    E("pe", lambda e, c=c, half=half, bank=bank: e.matmul(
                        bank[:, :], yT[:, c * 128:(c + 1) * 128],
                        warena[:, WP_OFF + c * 1024 + half * 512:
                               WP_OFF + c * 1024 + half * 512 + 512],
                        start=(c == 0), stop=(c == 7)),
                      reads=("yT", "w_proj"), writes=(btag,))
            E("dve", lambda e: e.tensor_copy(ao[:, 0:512], pjq0[:, :]),
              reads=("pjq0",), writes=("ao",))
            E("act", lambda e: e.activation(ao[:, 512:1024], pjq1[:, :], AF.Copy),
              reads=("pjq1",), writes=("ao",))
            ln_stats("ao", ao[:], setB)
            ln_norm("ao", ao[:], "lnout", lnout[:], setB)
            E("dve", lambda e, x1b=x1b, xtb=xtb: e.tensor_add(x1b[:], lnout[:], xtb[:]),
              reads=("lnout", xtag), writes=(x1tag,))
            E("sync", lambda e, s=s, t=t, x1b=x1b: e.dma_start(
                out=x1_d[s, t * 128:(t + 1) * 128, :], in_=x1b[:]),
              reads=(x1tag,), writes=(f"x1s_{s}_{t}",), dma=True,
              dsem=f"d_x1w{glob % 2}")

        def ln3_B(s, t, glob):
            p = glob % 2
            ln_stats(f"x1t{p}", x1t[p][:], f"A{p}")
            ln_norm(f"x1t{p}", x1t[p][:], "ln1", ln1[:], f"A{p}")
            transposes("ln1", lambda c: ln1[:, c * 128:(c + 1) * 128], 8,
                       f"lt{p}", lt[p])

        def tile_B(s, t, glob, nxt=None):
            p = glob % 2
            x1b, x1tag = x1t[p], f"x1t{p}"
            otb, otag = xt[p], f"xt{p}"
            ltb, lttag = lt[p], f"lt{p}"
            hTb, hTt = hT[p], "hT"
            setA, setB = f"A{p}", f"B{p}"

            if STAGE < 5 and STAGE not in (45, 46, 47):
                E("dve", lambda e: e.tensor_copy(otb[:], x1b[:]),
                  reads=(x1tag,), writes=(otag,))
                E("sync", lambda e, s=s, t=t, otb=otb: e.dma_start(
                    out=y_d[s, t * 128:(t + 1) * 128, :], in_=otb[:]),
                  reads=(otag,), writes=(f"y_{s}_{t}",), dma=True,
                  dsem=f"d_yw{glob % 2}")
                return

            if glob == 0:
                ln3_B(s, t, glob)

            psAVb = psAV[:, :].bitcast(BF16)
            def trelu(m):
                slot = m % 4
                if (m // 4) % 2 == 0:
                    bap, btg = ptr[:, slot * 128:(slot + 1) * 128], "ptr"
                else:
                    bap, btg = psAVb[:, slot * 128:(slot + 1) * 128], "psAV"
                E("pe", lambda e, m=m, bap=bap: e.transpose(
                    bap, hTb[:, m * 128:(m + 1) * 128], ident[:]),
                  reads=(f"{hTt}_{m}", "ident"), writes=(btg,))
                E("dve", lambda e, m=m, bap=bap: e.tensor_scalar(
                    hTb[:, m * 128:(m + 1) * 128], bap,
                    b1f[:, m:m + 1], 0.0, OP.add, OP.max),
                  reads=(btg, "b1f"), writes=(f"{hTt}_{m}",))

            for n in range(8):
                bank, btag = [(pjq0, "pjq0"), (pjq1, "pjq1"), (psE, "psE"),
                              (pjk, "pjk")][n % 4]
                for c in range(8):
                    blk = c * 8 + n
                    E("pe", lambda e, c=c, n=n, bank=bank, blk=blk: e.matmul(
                        bank[:, :], ltb[:, c * 128:(c + 1) * 128],
                        warena[:, W1_OFF + blk * 512:W1_OFF + blk * 512 + 512],
                        start=(c == 0), stop=(c == 7)),
                      reads=(lttag, f"w_w1_{blk // 16}"), writes=(btag,))
                E("dve", lambda e, n=n, bank=bank: e.tensor_copy(
                    hTb[:, n * 512:(n + 1) * 512], bank[:, :]),
                  reads=(btag,), writes=tuple(f"{hTt}_{4 * n + k}" for k in range(4)))
                if n >= 1:
                    for m in range(4 * (n - 1), 4 * n):
                        trelu(m)
            for m in range(28, 32):
                trelu(m)

            if nxt is not None:
                ln3_B(*nxt)

            for half, bank, btag in ((0, psO, "psO"), (1, psX, "psX")):
                for m in range(MFF):
                    E("pe", lambda e, m=m, half=half, bank=bank: e.matmul(
                        bank[:, :], hTb[:, m * 128:(m + 1) * 128],
                        warena[:, W2_OFF + m * 1024 + half * 512:
                               W2_OFF + m * 1024 + half * 512 + 512],
                        start=(m == 0), stop=False),
                      reads=(f"{hTt}_{m}", f"w_w2_{m // 8}"), writes=(btag,))
                E("pe", lambda e, half=half, bank=bank: e.matmul(
                    bank[:, :], onesr[:, :], b2r[:, half * 512:half * 512 + 512],
                    start=False, stop=True),
                  reads=("onesr", "b2r"), writes=(btag,))
                E("dve", lambda e, half=half, bank=bank: e.tensor_copy(
                    ff[:, half * 512:half * 512 + 512], bank[:, :]),
                  reads=(btag,), writes=("ff",))

            ln_stats("ff", ff[:], setB)
            ln_norm("ff", ff[:], "lnout", lnout[:], setB)
            E("dve", lambda e, otb=otb, x1b=x1b: e.tensor_add(otb[:], lnout[:], x1b[:]),
              reads=("lnout", x1tag), writes=(otag,))
            E("sync", lambda e, s=s, t=t, otb=otb: e.dma_start(
                out=y_d[s, t * 128:(t + 1) * 128, :], in_=otb[:]),
              reads=(otag,), writes=(f"y_{s}_{t}",), dma=True,
              dsem=f"d_yw{glob % 2}")

        tilesA = [(s, t) for s in range(seqs) for t in range(nt)]
        tilesB = list(tilesA)

        def load_A(idx):
            s, t = tilesA[idx]
            b = idx % 2
            E("sync", lambda e, s=s, t=t, b=b: e.dma_start(
                out=xt[b][:], in_=x_d[s, t * 128:(t + 1) * 128, :]),
              reads=(), writes=(f"xt{b}",), dma=True, dsem=f"d_xl{b}")

        def load_B(idx):
            s, t = tilesB[idx]
            b = idx % 2
            E("sync", lambda e, s=s, t=t, b=b: e.dma_start(
                out=x1t[b][:], in_=x1_d[s, t * 128:(t + 1) * 128, :]),
              reads=(f"x1s_{s}_{t}",), writes=(f"x1t{b}",), dma=True,
              dsem=f"d_x1l{b}")

        load_A(0)
        if len(tilesA) > 1:
            load_A(1)
        if STAGE >= 1 and STAGE not in (45, 46, 47) and STAGE != 0:
            s0_, t0_ = tilesA[0]
            ln1_A(s0_, t0_, 0)
            for chunk in kvq_A(s0_, t0_, 0):
                chunk()
        for i, (s, t) in enumerate(tilesA):
            nxt = (tilesA[i + 1][0], tilesA[i + 1][1], i + 1) if i + 1 < len(tilesA) else None
            tile_A(s, t, i, nxt)
            if i + 2 < len(tilesA):
                load_A(i + 2)
        w2overlap = [("w_kv", "w_q"), ("w_q", "w_proj"), ("w_proj",), ()]
        for i in range(4):
            E("pool", lambda e, i=i: e.dma_start(
                out=warena[:, W2_OFF + i * 8192:W2_OFF + (i + 1) * 8192],
                in_=w2_d[:, i * 8192:(i + 1) * 8192]),
              reads=(), writes=(f"w_w2_{i}",) + w2overlap[i], dma=True,
              dsem="d_w2", dpolicy="all")
        load_B(0)
        if len(tilesB) > 1:
            load_B(1)
        for i, (s, t) in enumerate(tilesB):
            nxt = (tilesB[i + 1][0], tilesB[i + 1][1], i + 1) if i + 1 < len(tilesB) else None
            tile_B(s, t, i, nxt)
            if i + 2 < len(tilesB):
                load_B(i + 2)

        items = tl.finalize()
        totals = dict(tl.counters)
        eng_dsems = {}
        for op in tl.ops:
            if op["dma"]:
                eng_dsems.setdefault(op["eng"], set()).add(op["dsem"])

        def replay(eng):
            def run(e):
                for it in items[eng]:
                    if it[0] == "wait":
                        e.wait_ge(sems[it[1]], it[2])
                    elif it[0] == "drain":
                        e.drain()
                    else:
                        op = it[1]
                        inst = op["fn"](e)
                        if op.get("post_drain") and op["ev"] is not None:
                            e.drain().then_inc(sems[op["ev"][0]], 1)
                        elif op["ev"] is not None:
                            step = 16 if op["dma"] else 1
                            inst.then_inc(sems[op["ev"][0]], step)
                for dsem in sorted(eng_dsems.get(eng, ())):
                    e.wait_ge(sems[dsem], totals[dsem])
            return run

        with nc.Block() as block:
            block.sync(replay("sync"))
            block.tensor(replay("pe"))
            block.scalar(replay("act"))
            block.vector(replay("dve"))
            block.gpsimd(replay("pool"))

    return nc



def _host_prep(Wq, Wkv, Wproj, w1, b1, w2, b2, nt=NT):
    f32 = np.float32
    wqT = np.ascontiguousarray(Wq.T).astype(f32, copy=False)
    wqT = wqT.reshape(D, H, HD)[:, PERM, :].reshape(D, D)
    wq_l = np.ascontiguousarray(wqT.reshape(8, 128, D).transpose(1, 0, 2).reshape(128, 8192))
    wkvT = np.ascontiguousarray(Wkv.T).astype(f32, copy=False)
    wkv_l = np.ascontiguousarray(wkvT.reshape(8, 128, 512).transpose(1, 0, 2).reshape(128, 4096))
    wpT = np.ascontiguousarray(Wproj.T).astype(f32, copy=False)
    wpT = wpT.reshape(H, HD, D)[PERM, :, :].reshape(D, D)
    wp_l = np.ascontiguousarray(wpT.reshape(8, 128, D).transpose(1, 0, 2).reshape(128, 8192))
    w1_l = np.ascontiguousarray(
        w1.reshape(8, 128, 8, 512).transpose(1, 0, 2, 3).reshape(128, 32768))
    w2_l = np.ascontiguousarray(
        w2.reshape(MFF, 128, D).transpose(1, 0, 2).reshape(128, 32768))
    b1_l = np.ascontiguousarray(b1.reshape(MFF, 128).T)
    b2_l = np.ascontiguousarray(b2[None, :])

    theta = (1.0 / 10000.0 ** (np.arange(0, HD, 2, dtype=f32) / HD)).astype(f32)
    pos = np.arange(nt * 128, dtype=f32)
    freqs = np.outer(pos, theta).astype(f32)
    cosv = np.cos(freqs)
    sinv = np.sin(freqs)
    cos_full = np.concatenate([cosv, cosv], axis=1)
    sin_s = np.concatenate([-sinv, sinv], axis=1)
    cos_l = np.ascontiguousarray(
        cos_full.reshape(nt, 128, HD).transpose(1, 0, 2).reshape(128, nt * HD)).astype(f32)
    sin_l = np.ascontiguousarray(
        sin_s.reshape(nt, 128, HD).transpose(1, 0, 2).reshape(128, nt * HD)).astype(f32)

    ident = np.eye(128, dtype=ml_dtypes.bfloat16)
    p = np.arange(128)
    m = np.arange(128)
    sel0 = (m[None, :] == (p % 64)[:, None]).astype(ml_dtypes.bfloat16)
    sel1 = (m[None, :] == 64 + (p % 64)[:, None]).astype(ml_dtypes.bfloat16)
    BIG = np.float32(-65536.0)
    mself = np.where(np.arange(128)[None, :] > np.arange(128)[:, None], BIG, 0.0)
    mold = np.where(np.arange(128)[None, :] <= np.arange(128)[:, None], BIG, 0.0)
    msk0s = mself[(p % 64), :].astype(ml_dtypes.bfloat16)
    msk1s = mself[64 + (p % 64), :].astype(ml_dtypes.bfloat16)
    msk0o = mold[(p % 64), :].astype(ml_dtypes.bfloat16)
    msk1o = mold[64 + (p % 64), :].astype(ml_dtypes.bfloat16)
    selmsk = np.ascontiguousarray(
        np.concatenate([sel0, sel1, msk0s, msk1s, msk0o, msk1o], axis=1))

    return dict(wq=wq_l, wkv=wkv_l, wp=wp_l, w1=w1_l, w2=w2_l, b1f=b1_l, b2bc=b2_l,
                cost=cos_l, sint=sin_l, ident=ident, selmsk=selmsk)


def kernel(L0_attn_x, L0_attn_Wq, L0_attn_Wkv, L0_attn_Wproj,
           L0_ffn_w1, L0_ffn_b1, L0_ffn_w2, L0_ffn_b2):
    x = np.ascontiguousarray(np.asarray(L0_attn_x, dtype=np.float32))
    B = x.shape[0]
    common = _host_prep(np.asarray(L0_attn_Wq, np.float32),
                        np.asarray(L0_attn_Wkv, np.float32),
                        np.asarray(L0_attn_Wproj, np.float32),
                        np.asarray(L0_ffn_w1, np.float32),
                        np.asarray(L0_ffn_b1, np.float32),
                        np.asarray(L0_ffn_w2, np.float32),
                        np.asarray(L0_ffn_b2, np.float32))

    nc = build_nc()
    in_maps = []
    for c in range(NCORES):
        m = dict(common)
        m["x"] = np.ascontiguousarray(x[c * SEQS:(c + 1) * SEQS])
        in_maps.append(m)
    res = run_bass_kernel_spmd(nc, in_maps, core_ids=list(range(NCORES)))
    out = np.concatenate([r["y"] for r in res.results], axis=0)
    return out


# version 13
# speedup vs baseline: 1.1238x; 1.1194x over previous
import contextlib
import numpy as np
import ml_dtypes

import concourse.bass as bass
import concourse.mybir as mybir
from concourse.bass_utils import run_bass_kernel_spmd

F32 = mybir.dt.float32
BF16 = mybir.dt.bfloat16
AF = mybir.ActivationFunctionType
OP = mybir.AluOpType
AX = mybir.AxisListType

NCORES = 8
SEQS = 2
T = 2048
D = 1024
NT = T // 128
H = 16
KVH = 4
HD = 64
WIN = 512
DFF = 4096
MFF = DFF // 128
LNEPS = 1e-5
QKEPS = 1e-6
STAGE = 5
INTERLEAVE = True
NOPAIRS = False
NOKVQ = False
SUB = 99
QKMODE = 'full'

PERM = [0, 4, 1, 5, 2, 6, 3, 7, 8, 12, 9, 13, 10, 14, 11, 15]
GOF = [PERM[s] // 4 for s in range(H)]

W1_OFF = 0
W2_OFF = 32768
WKV_OFF = W2_OFF
WQ_OFF = WKV_OFF + 4096
WP_OFF = WQ_OFF + 8192
WARENA_N = 65536

ENGINES = ("pe", "act", "dve", "pool", "sync")
DMA_SEM = {"sync": "d_sync", "pool": "d_pool"}
CMP_SEM = {"pe": "c_pe", "act": "c_act", "dve": "c_dve", "pool": "c_pool"}


class TL:

    def __init__(self):
        self.ops = []

    def emit(self, eng, fn, reads=(), writes=(), dma=False, post_drain=False,
             dsem=None, dpolicy="slot"):
        op = {
            "eng": eng, "fn": fn, "reads": tuple(reads), "writes": tuple(writes),
            "dma": dma, "idx": len(self.ops), "ev": None, "need_ev": dma,
            "post_drain": post_drain, "dsem": dsem, "dpolicy": dpolicy,
        }
        self.ops.append(op)
        return op

    def finalize(self):
        last_w = {}
        last_r = {}
        for op in self.ops:
            deps = {}
            for t in op["reads"]:
                w = last_w.get(t)
                if w is not None:
                    deps[w["idx"]] = w
            for t in op["writes"]:
                w = last_w.get(t)
                if w is not None and not (w["dma"] and op["dma"]):
                    deps[w["idx"]] = w
                for r in last_r.get(t, {}).values():
                    deps[r["idx"]] = r
            op["deps"] = [self.ops[i] for i in sorted(deps)]
            for t in op["reads"]:
                last_r.setdefault(t, {})[op["eng"]] = op
            for t in op["writes"]:
                last_w[t] = op
                last_r[t] = {}
        for op in self.ops:
            for d in op["deps"]:
                if d["dma"] or d["eng"] != op["eng"] or op["eng"] != "pe":
                    d["need_ev"] = True
        counters = {}
        for op in self.ops:
            if op["dma"]:
                sem = op["dsem"]
                counters[sem] = counters.get(sem, 0) + 16
                op["ev"] = (sem, counters[sem])
            elif op["need_ev"]:
                sem = CMP_SEM[op["eng"]]
                counters[sem] = counters.get(sem, 0) + 1
                op["ev"] = (sem, counters[sem])
        for op in self.ops:
            if op["dma"] and op["dpolicy"] == "all":
                op["ev"] = (op["dsem"], counters[op["dsem"]])
        items = {e: [] for e in ENGINES}
        watermark = {}
        last_drain = {e: -1 for e in ENGINES}
        for op in self.ops:
            eng = op["eng"]
            waits = {}
            need_drain = False
            for d in op["deps"]:
                if d["eng"] == "pe" and eng == "pe" and not d["dma"] and not op["dma"]:
                    continue
                if d["eng"] == eng and not d["dma"]:
                    if d["idx"] > last_drain[eng]:
                        need_drain = True
                    continue
                sem, val = d["ev"]
                if waits.get(sem, 0) < val:
                    waits[sem] = val
            if need_drain:
                items[eng].append(("drain",))
                last_drain[eng] = op["idx"] - 1
            for sem, val in sorted(waits.items()):
                if watermark.get((eng, sem), 0) < val:
                    items[eng].append(("wait", sem, val))
                    watermark[(eng, sem)] = val
            items[eng].append(("op", op))
        self.counters = counters
        return items


def ceil_div(a, b):
    return -(-a // b)


def build_nc(nt=NT, seqs=SEQS):
    nc = bass.Bass("TRN2", target_bir_lowering=False)
    Tq = nt * 128

    x_d = nc.dram_tensor("x", [seqs, Tq, D], F32, kind="ExternalInput").ap()
    wq_d = nc.dram_tensor("wq", [128, 8192], F32, kind="ExternalInput").ap()
    wkv_d = nc.dram_tensor("wkv", [128, 4096], F32, kind="ExternalInput").ap()
    wp_d = nc.dram_tensor("wp", [128, 8192], F32, kind="ExternalInput").ap()
    w1_d = nc.dram_tensor("w1", [128, 32768], F32, kind="ExternalInput").ap()
    w2_d = nc.dram_tensor("w2", [128, 32768], F32, kind="ExternalInput").ap()
    b1_d = nc.dram_tensor("b1f", [128, MFF], F32, kind="ExternalInput").ap()
    b2_d = nc.dram_tensor("b2bc", [1, D], F32, kind="ExternalInput").ap()
    cos_d = nc.dram_tensor("cost", [128, nt * HD], F32, kind="ExternalInput").ap()
    sin_d = nc.dram_tensor("sint", [128, nt * HD], F32, kind="ExternalInput").ap()
    ident_d = nc.dram_tensor("ident", [128, 128], BF16, kind="ExternalInput").ap()
    selmsk_d = nc.dram_tensor("selmsk", [128, 6 * 128], BF16, kind="ExternalInput").ap()
    x1_d = nc.dram_tensor("x1s", [seqs, Tq, D], F32, kind="Internal").ap()
    y_d = nc.dram_tensor("y", [seqs, Tq, D], F32, kind="ExternalOutput").ap()

    tl = TL()

    with contextlib.ExitStack() as ctx:
        sb = lambda name, shape, dt: ctx.enter_context(nc.sbuf_tensor("sb_" + name, shape, dt))
        ps = lambda name, shape, dt: ctx.enter_context(nc.psum_tensor(name, shape, dt))

        warena = sb("warena", [128, WARENA_N], BF16)
        cost = sb("cost", [128, nt * HD], BF16)
        sint = sb("sint", [128, nt * HD], BF16)
        b1f = sb("b1f", [128, MFF], F32)
        ident = sb("ident", [128, 128], BF16)
        selmsk = sb("selmsk", [128, 6 * 128], BF16)
        xt = [sb("xt0", [128, D], F32), sb("xt1", [128, D], F32)]
        x1t = [sb("x1t0", [128, D], F32), sb("x1t1", [128, D], F32)]
        junk = sb("junk", [128, D], BF16)
        sqr = sb("sqr", [128, D], BF16)
        lnout = sb("lnout", [128, D], F32)
        ao = sb("ao", [128, D], F32)
        ln1 = sb("ln1", [128, D], BF16)
        lt = [sb("lt0", [128, D], BF16), sb("lt1", [128, D], BF16)]
        qf = sb("qf", [128, D], BF16)
        qr = sb("qr", [128, D], BF16)
        qT = [sb("qT0", [128, D], BF16), sb("qT1", [128, D], BF16)]
        kf = sb("kf", [128, 256], BF16)
        kr4 = sb("kr4", [128, 256], BF16)
        krb = sb("krb", [128, 256], BF16)
        KT = sb("KT", [128, 6 * 256], BF16)
        VR = sb("VR", [128, 6 * 256], BF16)
        _a0 = [sb("aT00", [128, 640], BF16), sb("aT01", [128, 640], BF16)]
        Eb = [[sb("E00", [128, 640], BF16), sb("E01", [128, 640], BF16)],
              [sb("E10", [128, 640], BF16), sb("E11", [128, 640], BF16)]]
        aT = [_a0, _a0]
        yT = sb("yT", [128, D], BF16)
        _h0 = sb("hT0", [128, DFF], BF16)
        hT = [_h0, _h0]
        ff = sb("ff", [128, D], F32)
        st = sb("st", [128, 128], F32)
        b2r = sb("b2r", [1, D], BF16)
        onesr = sb("onesr", [1, 128], BF16)
        epsc = sb("epsc", [128, 1], F32)

        pjq0 = ps("pjq0", [128, 512], F32)
        pjq1 = ps("pjq1", [128, 512], F32)
        pjk = ps("pjk", [128, 512], F32)
        psE = ps("psE", [128, 512], F32)
        psO = ps("psO", [128, 512], F32)
        psAV = ps("psAV", [128, 512], F32)
        psX = ps("psX", [128, 512], F32)
        ptr = ps("ptr", [128, 1024], BF16)

        sems = {}
        for s in ("c_pe", "c_act", "c_dve", "c_pool", "d_wc", "d_w1", "d_w2",
                  "d_xl0", "d_xl1", "d_x1l0", "d_x1l1", "d_x1w0", "d_x1w1",
                  "d_yw0", "d_yw1"):
            sems[s] = ctx.enter_context(nc.semaphore(s))

        E = tl.emit

        SETBASE = {"A0": 0, "A1": 8, "B0": 16, "B1": 24}
        LNC = {"s1": 0, "s2": 1, "nm": 2, "t2": 3, "lnv": 4, "rstd": 5}

        def lnc(set_, name):
            return st[:, SETBASE[set_] + LNC[name]:SETBASE[set_] + LNC[name] + 1]

        def qkc(p, name, w=1):
            base = 32 + 40 * p + {"qss": 0, "qn": 16, "kss": 32, "kn": 36}[name]
            return st[:, base:base + w]

        def dnc(cp, i, name, w=1):
            base = 112 + cp * 8 + {"da": 0, "db": 2, "den": 4, "rc": 6,
                                   "lnb": 6}[name] + i
            return st[:, base:base + w]

        E("dve", lambda e: e.memset(epsc[:], LNEPS), writes=("epsc",))
        E("dve", lambda e: e.memset(onesr[:], 1.0), writes=("onesr",))
        E("pool", lambda e: e.dma_start(out=ident[:], in_=ident_d), writes=("ident",), dma=True, dsem="d_wc", dpolicy="all")
        E("pool", lambda e: e.dma_start(out=selmsk[:], in_=selmsk_d), writes=("selmsk",), dma=True, dsem="d_wc", dpolicy="all")
        E("pool", lambda e: e.dma_start(out=cost[:], in_=cos_d), writes=("cost",), dma=True, dsem="d_wc", dpolicy="all")
        E("pool", lambda e: e.dma_start(out=sint[:], in_=sin_d), writes=("sint",), dma=True, dsem="d_wc", dpolicy="all")
        E("pool", lambda e: e.dma_start(out=b1f[:], in_=b1_d), writes=("b1f",), dma=True, dsem="d_wc", dpolicy="all")
        E("pool", lambda e: e.dma_start(out=b2r[:], in_=b2_d), writes=("b2r",), dma=True, dsem="d_wc", dpolicy="all")
        E("pool", lambda e: e.dma_start(out=warena[:, WKV_OFF:WKV_OFF + 4096], in_=wkv_d),
          writes=("w_kv", "w_w2_0"), dma=True, dsem="d_wc", dpolicy="all")
        E("pool", lambda e: e.dma_start(out=warena[:, WQ_OFF:WQ_OFF + 8192], in_=wq_d),
          writes=("w_q", "w_w2_0", "w_w2_1"), dma=True, dsem="d_wc", dpolicy="all")
        E("pool", lambda e: e.dma_start(out=warena[:, WP_OFF:WP_OFF + 8192], in_=wp_d),
          writes=("w_proj", "w_w2_1", "w_w2_2"), dma=True, dsem="d_wc", dpolicy="all")
        for i in range(4):
            E("pool", lambda e, i=i: e.dma_start(
                out=warena[:, W1_OFF + i * 8192:W1_OFF + (i + 1) * 8192],
                in_=w1_d[:, i * 8192:(i + 1) * 8192]), writes=(f"w_w1_{i}",),
              dma=True, dsem="d_w1", dpolicy="all")

        def ln_stats(src_tag, src_ap, set_):
            tg = lambda n: f"st_{set_}_{n}"
            E("act", lambda e: e.activation(junk[:], src_ap, AF.Square, scale=1.0 / 32.0,
                                            accum_out=lnc(set_, "s2")),
              reads=(src_tag,), writes=(tg("s2"), "jkJ", "jkS"))
            E("dve", lambda e: e.tensor_reduce(lnc(set_, "s1"), src_ap, AX.X, OP.add),
              reads=(src_tag,), writes=(tg("s1"),))
            E("dve", lambda e: e.tensor_scalar(lnc(set_, "nm"), lnc(set_, "s1"),
                                               -1.0 / D, None, OP.mult),
              reads=(tg("s1"),), writes=(tg("nm"),))
            E("dve", lambda e: e.scalar_tensor_tensor(
                lnc(set_, "t2"), lnc(set_, "nm"), lnc(set_, "nm"), lnc(set_, "s2"),
                OP.mult, OP.subtract),
              reads=(tg("nm"), tg("s2")), writes=(tg("t2"),))
            E("act", lambda e: e.activation(lnc(set_, "lnv"), lnc(set_, "t2"), AF.Ln,
                                            scale=-1.0, bias=epsc[:]),
              reads=(tg("t2"), "epsc"), writes=(tg("lnv"),))
            E("act", lambda e: e.activation(lnc(set_, "rstd"), lnc(set_, "lnv"),
                                            AF.Exp, scale=-0.5),
              reads=(tg("lnv"),), writes=(tg("rstd"),))

        def ln_norm(src_tag, src_ap, out_tag, out_ap, set_):
            E("dve", lambda e: e.tensor_scalar(out_ap, src_ap, lnc(set_, "nm"),
                                               lnc(set_, "rstd"), OP.add, OP.mult),
              reads=(src_tag, f"st_{set_}_nm", f"st_{set_}_rstd"), writes=(out_tag,))

        def transposes(src_tag, src_ap_fn, nchunks, dst_tag, dst_sb):
            for c in range(nchunks):
                E("pe", lambda e, c=c: e.transpose(
                    ptr[:, c * 128:(c + 1) * 128], src_ap_fn(c), ident[:]),
                  reads=(src_tag, "ident"), writes=("ptr",))
            off = 0
            while off < nchunks * 128:
                w = min(512, nchunks * 128 - off)
                E("dve", lambda e, off=off, w=w: e.tensor_copy(
                    dst_sb[:, off:off + w], ptr[:, off:off + w]),
                  reads=("ptr",), writes=(dst_tag,))
                off += w

        def ln1_A(s, t, glob):
            p = glob % 2
            ln_stats(f"xt{p}", xt[p][:], f"A{p}")
            ln_norm(f"xt{p}", xt[p][:], "ln1", ln1[:], f"A{p}")
            transposes("ln1", lambda c: ln1[:, c * 128:(c + 1) * 128], 8,
                       f"lt{p}", lt[p])

        def kvq_A(s, t, glob):
            p = glob % 2
            ltb, lttag = lt[p], f"lt{p}"
            qTb, qTtag = qT[p], f"qT{p}"
            slot = glob % 6
            cslice = lambda j0, w: cost[:, t * HD + j0: t * HD + j0 + w]
            sslice = lambda j0, w: sint[:, t * HD + j0: t * HD + j0 + w]

            def kv_mms():
                for c in range(8):
                    E("pe", lambda e, c=c: e.matmul(
                        pjq1[:, :], ltb[:, c * 128:(c + 1) * 128],
                        warena[:, WKV_OFF + c * 512:WKV_OFF + (c + 1) * 512],
                        start=(c == 0), stop=(c == 7)),
                      reads=(lttag, "w_kv"), writes=("pjq1",))

            def k_chain():
                E("act", lambda e: e.activation(sqr[:, 0:256], pjq1[:, 0:256], AF.Square),
                  reads=("pjq1",), writes=("sqr",))
                E("dve", lambda e: e.tensor_reduce(
                    qkc(p, "kss", 4), sqr[:, 0:256].rearrange("p (g w) -> p g w", g=4),
                    AX.X, OP.add),
                  reads=("sqr",), writes=(f"st_kss{p}",))
                E("act", lambda e: e.activation(qkc(p, "kss", 4), qkc(p, "kss", 4), AF.Ln),
                  reads=(f"st_kss{p}",), writes=(f"st_kss{p}",))
                E("act", lambda e: e.activation(qkc(p, "kss", 4), qkc(p, "kss", 4),
                                                AF.Exp, scale=0.5),
                  reads=(f"st_kss{p}",), writes=(f"st_kss{p}",))
                E("dve", lambda e: e.tensor_scalar(qkc(p, "kss", 4), qkc(p, "kss", 4),
                                                   QKEPS, None, OP.add),
                  reads=(f"st_kss{p}",), writes=(f"st_kss{p}",))
                E("dve", lambda e: e.reciprocal(qkc(p, "kss", 4), qkc(p, "kss", 4)),
                  reads=(f"st_kss{p}",), writes=(f"st_kss{p}",))
                E("dve", lambda e: e.tensor_tensor(
                    kf[:].rearrange("p (g w) -> p g w", g=4),
                    pjq1[:, 0:256].rearrange("p (g w) -> p g w", g=4),
                    qkc(p, "kss", 4).unsqueeze(2).broadcast_to((128, 4, HD)), OP.mult),
                  reads=("pjq1", f"st_kss{p}"), writes=("kf",))
                E("dve", lambda e: e.tensor_copy(
                    VR[:, slot * 256:slot * 256 + 256], pjq1[:, 256:512]),
                  reads=("pjq1",), writes=(f"V{slot}",))
                E("dve", lambda e: e.tensor_tensor(
                    kr4[:].rearrange("p (g w) -> p g w", g=4)[:, :, 0:32],
                    kf[:].rearrange("p (g w) -> p g w", g=4)[:, :, 32:64],
                    sslice(0, 32).unsqueeze(1).broadcast_to((128, 4, 32)), OP.mult),
                  reads=("kf", "sint"), writes=("kr4",))
                E("dve", lambda e: e.tensor_tensor(
                    kr4[:].rearrange("p (g w) -> p g w", g=4)[:, :, 32:64],
                    kf[:].rearrange("p (g w) -> p g w", g=4)[:, :, 0:32],
                    sslice(32, 32).unsqueeze(1).broadcast_to((128, 4, 32)), OP.mult),
                  reads=("kf", "sint"), writes=("kr4",))
                E("dve", lambda e: e.tensor_tensor(
                    krb[:].rearrange("p (g w) -> p g w", g=4),
                    kf[:].rearrange("p (g w) -> p g w", g=4),
                    cslice(0, HD).unsqueeze(1).broadcast_to((128, 4, HD)), OP.mult),
                  reads=("kf", "cost"), writes=("krb",))
                E("dve", lambda e: e.tensor_add(krb[:], krb[:], kr4[:]),
                  reads=("krb", "kr4"), writes=("krb",))

            def kT_tr():
                for c in range(2):
                    E("pe", lambda e, c=c: e.transpose(
                        ptr[:, c * 128:(c + 1) * 128], krb[:, c * 128:(c + 1) * 128],
                        ident[:]),
                      reads=("krb", "ident"), writes=("ptr",))
                E("dve", lambda e: e.tensor_copy(
                    KT[:, slot * 256:slot * 256 + 256], ptr[:, 0:256]),
                  reads=("ptr",), writes=(f"K{slot}",))

            def q_half(half):
                import os
                q0mode = os.environ.get("Q0MODE", "full")
                for c in range(8):
                    E("pe", lambda e, c=c, half=half: e.matmul(
                        pjq0[:, :], ltb[:, c * 128:(c + 1) * 128],
                        warena[:, WQ_OFF + c * 1024 + half * 512:
                               WQ_OFF + c * 1024 + half * 512 + 512],
                        start=(c == 0), stop=(c == 7)),
                      reads=(lttag, "w_q"), writes=("pjq0",))
                if q0mode == "mmonly":
                    return
                E("act", lambda e, half=half: e.activation(
                    sqr[:, half * 512:half * 512 + 512], pjq0[:, :], AF.Square),
                  reads=("pjq0",), writes=("sqr", "q0tok"))
                if q0mode == "nocopy":
                    return
                E("dve", lambda e, half=half: e.tensor_copy(
                    qf[:, half * 512:half * 512 + 512], pjq0[:, :]),
                  reads=("pjq0", "q0tok"), writes=("qf",))

            def q_chain():
                E("dve", lambda e: e.tensor_reduce(
                    qkc(p, "qn", 16), sqr[:].rearrange("p (g w) -> p g w", g=16),
                    AX.X, OP.add),
                  reads=("sqr",), writes=(f"st_qn{p}",))
                E("act", lambda e: e.activation(qkc(p, "qn", 16), qkc(p, "qn", 16), AF.Ln),
                  reads=(f"st_qn{p}",), writes=(f"st_qn{p}",))
                E("act", lambda e: e.activation(qkc(p, "qn", 16), qkc(p, "qn", 16),
                                                AF.Exp, scale=0.5),
                  reads=(f"st_qn{p}",), writes=(f"st_qn{p}",))
                E("dve", lambda e: e.tensor_scalar(qkc(p, "qn", 16), qkc(p, "qn", 16),
                                                   QKEPS, None, OP.add),
                  reads=(f"st_qn{p}",), writes=(f"st_qn{p}",))
                E("dve", lambda e: e.reciprocal(qkc(p, "qn", 16), qkc(p, "qn", 16)),
                  reads=(f"st_qn{p}",), writes=(f"st_qn{p}",))
                E("dve", lambda e: e.tensor_tensor(
                    qr[:].rearrange("p (g w) -> p g w", g=16),
                    qf[:].rearrange("p (g w) -> p g w", g=16),
                    qkc(p, "qn", 16).unsqueeze(2).broadcast_to((128, 16, HD)), OP.mult),
                  reads=("qf", f"st_qn{p}"), writes=("qr",))
                E("dve", lambda e: e.tensor_tensor(
                    sqr[:].rearrange("p (g w) -> p g w", g=16)[:, :, 0:32],
                    qr[:].rearrange("p (g w) -> p g w", g=16)[:, :, 32:64],
                    sslice(0, 32).unsqueeze(1).broadcast_to((128, 16, 32)), OP.mult),
                  reads=("qr", "sint"), writes=("sqr",))
                E("dve", lambda e: e.tensor_tensor(
                    sqr[:].rearrange("p (g w) -> p g w", g=16)[:, :, 32:64],
                    qr[:].rearrange("p (g w) -> p g w", g=16)[:, :, 0:32],
                    sslice(32, 32).unsqueeze(1).broadcast_to((128, 16, 32)), OP.mult),
                  reads=("qr", "sint"), writes=("sqr",))
                E("dve", lambda e: e.tensor_tensor(
                    qf[:].rearrange("p (g w) -> p g w", g=16),
                    qr[:].rearrange("p (g w) -> p g w", g=16),
                    cslice(0, HD).unsqueeze(1).broadcast_to((128, 16, HD)), OP.mult),
                  reads=("qr", "cost"), writes=("qf",))
                E("dve", lambda e: e.tensor_add(qr[:], qf[:], sqr[:]),
                  reads=("qf", "sqr"), writes=("qr",))

            def qT_tr():
                transposes("qr", lambda c: qr[:, c * 128:(c + 1) * 128], 8, qTtag, qTb)

            import os
            mask = int(os.environ.get("KVQMASK", "127"))
            chunks = [kv_mms, k_chain, kT_tr, lambda: q_half(0), lambda: q_half(1),
                      q_chain, qT_tr]
            return [c for i, c in enumerate(chunks) if mask & (1 << i)]

        def tile_A(s, t, glob, nxt=None):
            p = glob % 2
            xtb, xtag = xt[p], f"xt{p}"
            x1b, x1tag = x1t[p], f"x1t{p}"
            qTb, qTtag = qT[p], f"qT{p}"
            setB = f"B{p}"

            def fallback_store():
                E("dve", lambda e: e.tensor_copy(x1b[:], xtb[:]),
                  reads=(xtag,), writes=(x1tag,))
                E("sync", lambda e: e.dma_start(
                    out=x1_d[s, t * 128:(t + 1) * 128, :], in_=x1b[:]),
                  reads=(x1tag,), writes=(f"x1s_{s}_{t}",), dma=True,
                  dsem=f"d_x1w{glob % 2}")

            if STAGE == 0 or STAGE in (45, 46, 47):
                fallback_store()
                return

            if nxt is not None:
                ln1_A(*nxt)
            kvq = kvq_A(*nxt) if nxt is not None else []

            j0 = max(0, t - 4)
            J = list(range(j0, t))
            na = len(J)
            mask_old = (t >= 4)

            def pair_phase1(c):
                cp = c % 2
                s0, s1_ = 2 * c, 2 * c + 1
                heads = []
                for i, (sl, bank, btag, base) in enumerate(
                        ((s0, psE, "psE", 0), (s1_, psO, "psO", 64))):
                    sbank, sbtag = (pjk, "pjk") if i == 0 else (psX, "psX")
                    heads.append(dict(
                        i=i, sl=sl, bank=bank, btag=btag, base=base, g=GOF[sl],
                        sbank=sbank, sbtag=sbtag, soff=cp * 128,
                        qap=qTb[base:base + 64, (sl // 2) * 128:(sl // 2) * 128 + 128],
                        eb=Eb[cp][i], etag=f"E{cp}{i}"))

                def kslice(hd, j):
                    sl6 = (glob - (t - j)) % 6
                    return KT[hd["base"]:hd["base"] + 64,
                              sl6 * 256 + (hd["g"] // 2) * 128:
                              sl6 * 256 + (hd["g"] // 2) * 128 + 128]

                for jj, j in enumerate(J):
                    for hd in heads:
                        bank, btag, qap = hd["bank"], hd["btag"], hd["qap"]
                        kap = kslice(hd, j)
                        if jj == 0 and mask_old:
                            E("pe", lambda e, bank=bank, qap=qap, kap=kap, jj=jj: e.matmul(
                                bank[:, jj * 128:(jj + 1) * 128], qap, kap,
                                start=True, stop=False),
                              reads=(qTtag, f"K{(glob - (t - j)) % 6}"), writes=(btag,))
                            for r in range(2):
                                selap = selmsk[hd["base"]:hd["base"] + 64,
                                               r * 128:(r + 1) * 128]
                                mskap = selmsk[hd["base"]:hd["base"] + 64,
                                               (4 + r) * 128:(5 + r) * 128]
                                E("pe", lambda e, bank=bank, selap=selap, mskap=mskap,
                                  jj=jj, r=r: e.matmul(
                                      bank[:, jj * 128:(jj + 1) * 128], selap, mskap,
                                      start=False, stop=(r == 1)),
                                  reads=("selmsk",), writes=(btag,))
                        else:
                            E("pe", lambda e, bank=bank, qap=qap, kap=kap, jj=jj: e.matmul(
                                bank[:, jj * 128:(jj + 1) * 128], qap, kap,
                                start=True, stop=True),
                              reads=(qTtag, f"K{(glob - (t - j)) % 6}"), writes=(btag,))
                for hd in heads:
                    sbank, sbtag, qap = hd["sbank"], hd["sbtag"], hd["qap"]
                    so = hd["soff"]
                    kap = kslice(hd, t)
                    E("pe", lambda e, sbank=sbank, qap=qap, kap=kap, so=so: e.matmul(
                        sbank[:, so:so + 128], qap, kap, start=True, stop=False),
                      reads=(qTtag, f"K{glob % 6}"), writes=(sbtag,))
                    for r in range(2):
                        selap = selmsk[hd["base"]:hd["base"] + 64, r * 128:(r + 1) * 128]
                        mskap = selmsk[hd["base"]:hd["base"] + 64,
                                       (2 + r) * 128:(3 + r) * 128]
                        E("pe", lambda e, sbank=sbank, selap=selap, mskap=mskap, r=r,
                          so=so: e.matmul(sbank[:, so:so + 128], selap, mskap,
                                          start=False, stop=(r == 1)),
                          reads=("selmsk",), writes=(sbtag,))
                for hd in heads:
                    i = hd["i"]
                    if na > 0:
                        E("act", lambda e, hd=hd, cp=cp: e.activation(
                            junk[:, 0:na * 128], hd["bank"][:, 0:na * 128],
                            AF.Exp, scale=0.125, accum_out=dnc(cp, hd["i"], "da")),
                          reads=(hd["btag"],), writes=(f"st_da{cp}{i}", "jkJ"))
                    E("act", lambda e, hd=hd, cp=cp: e.activation(
                        junk[:, 512:640], hd["sbank"][:, hd["soff"]:hd["soff"] + 128],
                        AF.Exp, scale=0.125, accum_out=dnc(cp, hd["i"], "db")),
                      reads=(hd["sbtag"],), writes=(f"st_db{cp}{i}", "jkS"))
                if na > 0:
                    E("dve", lambda e, cp=cp: e.tensor_tensor(
                        dnc(cp, 0, "den", 2), dnc(cp, 0, "da", 2),
                        dnc(cp, 0, "db", 2), OP.add),
                      reads=(f"st_da{cp}0", f"st_da{cp}1", f"st_db{cp}0",
                             f"st_db{cp}1"),
                      writes=(f"st_den{cp}0", f"st_den{cp}1"))
                else:
                    E("dve", lambda e, cp=cp: e.tensor_copy(
                        dnc(cp, 0, "den", 2), dnc(cp, 0, "db", 2)),
                      reads=(f"st_db{cp}0", f"st_db{cp}1"),
                      writes=(f"st_den{cp}0", f"st_den{cp}1"))
                E("dve", lambda e, cp=cp: e.reciprocal(
                    dnc(cp, 0, "rc", 2), dnc(cp, 0, "den", 2)),
                  reads=(f"st_den{cp}0", f"st_den{cp}1"),
                  writes=(f"st_rc{cp}0", f"st_rc{cp}1"))
                E("act", lambda e, cp=cp: e.activation(
                    dnc(cp, 0, "lnb", 2), dnc(cp, 0, "rc", 2), AF.Ln),
                  reads=(f"st_rc{cp}0", f"st_rc{cp}1"),
                  writes=(f"st_rc{cp}0", f"st_rc{cp}1"))
                for hd in heads:
                    i = hd["i"]
                    if na > 0:
                        E("act", lambda e, hd=hd, cp=cp: e.activation(
                            hd["eb"][:, 0:na * 128], hd["bank"][:, 0:na * 128],
                            AF.Exp, scale=0.125, bias=dnc(cp, hd["i"], "lnb")),
                          reads=(hd["btag"], f"st_rc{cp}{i}"), writes=(hd["etag"],))
                    E("act", lambda e, hd=hd, cp=cp: e.activation(
                        hd["eb"][:, na * 128:(na + 1) * 128],
                        hd["sbank"][:, hd["soff"]:hd["soff"] + 128],
                        AF.Exp, scale=0.125, bias=dnc(cp, hd["i"], "lnb")),
                      reads=(hd["sbtag"], f"st_rc{cp}{i}"), writes=(hd["etag"],))
                return heads

            def pair_phase2(c, heads):
                cp = c % 2
                nch = na + 1
                ev_slots = [0, 1, 2, 3, 4][:nch]
                od_slots = [5, 6, 7, 0, 1][:nch]
                for i, slots in ((0, ev_slots), (1, od_slots)):
                    hd = heads[i]
                    eb, etag = hd["eb"], hd["etag"]
                    atb, atag = aT[cp][i], f"aT{i}"
                    for jj, slot in enumerate(slots):
                        E("pe", lambda e, eb=eb, jj=jj, slot=slot: e.transpose(
                            ptr[:, slot * 128:(slot + 1) * 128],
                            eb[:, jj * 128:(jj + 1) * 128], ident[:]),
                          reads=(etag, "ident"), writes=("ptr",))
                    run = []
                    for jj, slot in enumerate(slots + [None]):
                        if run and (slot is None or slot != run[-1][1] + 1):
                            j0r, s0r = run[0]
                            w = len(run) * 128
                            E("dve", lambda e, atb=atb, j0r=j0r, s0r=s0r, w=w:
                              e.tensor_copy(atb[:, j0r * 128:j0r * 128 + w],
                                            ptr[:, s0r * 128:s0r * 128 + w]),
                              reads=("ptr",), writes=(atag,))
                            run = []
                        if slot is not None:
                            run.append((jj, slot))
                s0, s1_ = 2 * c, 2 * c + 1
                for i, sl in enumerate((s0, s1_)):
                    g = GOF[sl]
                    hd = heads[i]
                    atb, atag = aT[cp][i], f"aT{i}"
                    outap = psAV[i * 64:(i + 1) * 64,
                                 cp * 256 + i * 128:cp * 256 + (i + 1) * 128]
                    chunks = J + [t]
                    for jj, j in enumerate(chunks):
                        sl6 = (glob - (t - j)) % 6
                        vap = VR[:, sl6 * 256 + g * 64:sl6 * 256 + g * 64 + 64]
                        E("pe", lambda e, outap=outap, vap=vap, atb=atb, jj=jj, i=i,
                          first=(jj == 0), last=(jj == len(chunks) - 1): e.matmul(
                              outap, vap, atb[:, jj * 128:(jj + 1) * 128],
                              start=first, stop=last, tile_position=(0, i * 64)),
                          reads=(f"V{sl6}", atag), writes=("psAV",))
                E("dve", lambda e, c=c, cp=cp: e.tensor_copy(
                    yT[0:64, c * 128:(c + 1) * 128], psAV[0:64, cp * 256:cp * 256 + 128]),
                  reads=("psAV",), writes=("yT",))
                E("dve", lambda e, c=c, cp=cp: e.tensor_copy(
                    yT[64:128, c * 128:(c + 1) * 128],
                    psAV[64:128, cp * 256 + 128:cp * 256 + 256]),
                  reads=("psAV",), writes=("yT",))

            if NOKVQ:
                kvq = []
            sched = {1: [0], 2: [1], 3: [3], 4: [4], 5: [5], 6: [2]} if INTERLEAVE else {}
            prev = None
            if not NOPAIRS:
                for c in range(8):
                    heads = pair_phase1(c)
                    if prev is not None:
                        pair_phase2(prev[0], prev[1])
                    for k in sched.get(c, []):
                        if k < len(kvq):
                            kvq[k]()
                    prev = (c, heads)
                pair_phase2(prev[0], prev[1])
                if INTERLEAVE:
                    if len(kvq) > 6:
                        kvq[6]()
                else:
                    for chunk in kvq:
                        chunk()
            else:
                for chunk in kvq:
                    chunk()
                E("dve", lambda e: e.memset(yT[:], 0.0), writes=("yT",))

            if STAGE == 3:
                fallback_store()
                return

            for half, bank, btag in ((0, pjq0, "pjq0"), (1, pjq1, "pjq1")):
                for c in range(8):
                    E("pe", lambda e, c=c, half=half, bank=bank: e.matmul(
                        bank[:, :], yT[:, c * 128:(c + 1) * 128],
                        warena[:, WP_OFF + c * 1024 + half * 512:
                               WP_OFF + c * 1024 + half * 512 + 512],
                        start=(c == 0), stop=(c == 7)),
                      reads=("yT", "w_proj"), writes=(btag,))
            E("dve", lambda e: e.tensor_copy(ao[:, 0:512], pjq0[:, :]),
              reads=("pjq0",), writes=("ao",))
            E("act", lambda e: e.activation(ao[:, 512:1024], pjq1[:, :], AF.Copy),
              reads=("pjq1",), writes=("ao",))
            ln_stats("ao", ao[:], setB)
            ln_norm("ao", ao[:], "lnout", lnout[:], setB)
            E("dve", lambda e, x1b=x1b, xtb=xtb: e.tensor_add(x1b[:], lnout[:], xtb[:]),
              reads=("lnout", xtag), writes=(x1tag,))
            E("sync", lambda e, s=s, t=t, x1b=x1b: e.dma_start(
                out=x1_d[s, t * 128:(t + 1) * 128, :], in_=x1b[:]),
              reads=(x1tag,), writes=(f"x1s_{s}_{t}",), dma=True,
              dsem=f"d_x1w{glob % 2}")

        def ln3_B(s, t, glob):
            p = glob % 2
            ln_stats(f"x1t{p}", x1t[p][:], f"A{p}")
            ln_norm(f"x1t{p}", x1t[p][:], "ln1", ln1[:], f"A{p}")
            transposes("ln1", lambda c: ln1[:, c * 128:(c + 1) * 128], 8,
                       f"lt{p}", lt[p])

        def tile_B(s, t, glob, nxt=None):
            p = glob % 2
            x1b, x1tag = x1t[p], f"x1t{p}"
            otb, otag = xt[p], f"xt{p}"
            ltb, lttag = lt[p], f"lt{p}"
            hTb, hTt = hT[p], "hT"
            setA, setB = f"A{p}", f"B{p}"

            if STAGE < 5 and STAGE not in (45, 46, 47):
                E("dve", lambda e: e.tensor_copy(otb[:], x1b[:]),
                  reads=(x1tag,), writes=(otag,))
                E("sync", lambda e, s=s, t=t, otb=otb: e.dma_start(
                    out=y_d[s, t * 128:(t + 1) * 128, :], in_=otb[:]),
                  reads=(otag,), writes=(f"y_{s}_{t}",), dma=True,
                  dsem=f"d_yw{glob % 2}")
                return

            if glob == 0:
                ln3_B(s, t, glob)

            psAVb = psAV[:, :].bitcast(BF16)
            def trelu(m):
                slot = (m // 2) % 4
                if m % 2 == 0:
                    bap, btg = ptr[:, slot * 128:(slot + 1) * 128], "ptr"
                else:
                    bap, btg = psAVb[:, slot * 128:(slot + 1) * 128], "psAV"
                E("pe", lambda e, m=m, bap=bap: e.transpose(
                    bap, hTb[:, m * 128:(m + 1) * 128], ident[:]),
                  reads=(f"{hTt}_{m}", "ident"), writes=(btg,))
                E("dve", lambda e, m=m, bap=bap: e.tensor_scalar(
                    hTb[:, m * 128:(m + 1) * 128], bap,
                    b1f[:, m:m + 1], 0.0, OP.add, OP.max),
                  reads=(btg, "b1f"), writes=(f"{hTt}_{m}",))

            for n in range(8):
                bank, btag = [(pjq0, "pjq0"), (pjq1, "pjq1"), (psE, "psE"),
                              (pjk, "pjk")][n % 4]
                for c in range(8):
                    blk = c * 8 + n
                    E("pe", lambda e, c=c, n=n, bank=bank, blk=blk: e.matmul(
                        bank[:, :], ltb[:, c * 128:(c + 1) * 128],
                        warena[:, W1_OFF + blk * 512:W1_OFF + blk * 512 + 512],
                        start=(c == 0), stop=(c == 7)),
                      reads=(lttag, f"w_w1_{blk // 16}"), writes=(btag,))
                E("dve", lambda e, n=n, bank=bank: e.tensor_copy(
                    hTb[:, n * 512:(n + 1) * 512], bank[:, :]),
                  reads=(btag,), writes=tuple(f"{hTt}_{4 * n + k}" for k in range(4)))
                if n >= 1:
                    for m in range(4 * (n - 1), 4 * n):
                        trelu(m)
            for m in range(28, 32):
                trelu(m)

            if nxt is not None:
                ln3_B(*nxt)

            for half, bank, btag in ((0, psO, "psO"), (1, psX, "psX")):
                for m in range(MFF):
                    E("pe", lambda e, m=m, half=half, bank=bank: e.matmul(
                        bank[:, :], hTb[:, m * 128:(m + 1) * 128],
                        warena[:, W2_OFF + m * 1024 + half * 512:
                               W2_OFF + m * 1024 + half * 512 + 512],
                        start=(m == 0), stop=False),
                      reads=(f"{hTt}_{m}", f"w_w2_{m // 8}"), writes=(btag,))
                E("pe", lambda e, half=half, bank=bank: e.matmul(
                    bank[:, :], onesr[:, :], b2r[:, half * 512:half * 512 + 512],
                    start=False, stop=True),
                  reads=("onesr", "b2r"), writes=(btag,))
                E("dve", lambda e, half=half, bank=bank: e.tensor_copy(
                    ff[:, half * 512:half * 512 + 512], bank[:, :]),
                  reads=(btag,), writes=("ff",))

            ln_stats("ff", ff[:], setB)
            ln_norm("ff", ff[:], "lnout", lnout[:], setB)
            E("dve", lambda e, otb=otb, x1b=x1b: e.tensor_add(otb[:], lnout[:], x1b[:]),
              reads=("lnout", x1tag), writes=(otag,))
            E("sync", lambda e, s=s, t=t, otb=otb: e.dma_start(
                out=y_d[s, t * 128:(t + 1) * 128, :], in_=otb[:]),
              reads=(otag,), writes=(f"y_{s}_{t}",), dma=True,
              dsem=f"d_yw{glob % 2}")

        tilesA = [(s, t) for s in range(seqs) for t in range(nt)]
        tilesB = list(tilesA)

        def load_A(idx):
            s, t = tilesA[idx]
            b = idx % 2
            E("sync", lambda e, s=s, t=t, b=b: e.dma_start(
                out=xt[b][:], in_=x_d[s, t * 128:(t + 1) * 128, :]),
              reads=(), writes=(f"xt{b}",), dma=True, dsem=f"d_xl{b}")

        def load_B(idx):
            s, t = tilesB[idx]
            b = idx % 2
            E("sync", lambda e, s=s, t=t, b=b: e.dma_start(
                out=x1t[b][:], in_=x1_d[s, t * 128:(t + 1) * 128, :]),
              reads=(f"x1s_{s}_{t}",), writes=(f"x1t{b}",), dma=True,
              dsem=f"d_x1l{b}")

        load_A(0)
        if len(tilesA) > 1:
            load_A(1)
        if STAGE >= 1 and STAGE not in (45, 46, 47) and STAGE != 0:
            s0_, t0_ = tilesA[0]
            ln1_A(s0_, t0_, 0)
            for chunk in kvq_A(s0_, t0_, 0):
                chunk()
        for i, (s, t) in enumerate(tilesA):
            nxt = (tilesA[i + 1][0], tilesA[i + 1][1], i + 1) if i + 1 < len(tilesA) else None
            tile_A(s, t, i, nxt)
            if i + 2 < len(tilesA):
                load_A(i + 2)
        w2overlap = [("w_kv", "w_q"), ("w_q", "w_proj"), ("w_proj",), ()]
        for i in range(4):
            E("pool", lambda e, i=i: e.dma_start(
                out=warena[:, W2_OFF + i * 8192:W2_OFF + (i + 1) * 8192],
                in_=w2_d[:, i * 8192:(i + 1) * 8192]),
              reads=(), writes=(f"w_w2_{i}",) + w2overlap[i], dma=True,
              dsem="d_w2", dpolicy="all")
        load_B(0)
        if len(tilesB) > 1:
            load_B(1)
        for i, (s, t) in enumerate(tilesB):
            nxt = (tilesB[i + 1][0], tilesB[i + 1][1], i + 1) if i + 1 < len(tilesB) else None
            tile_B(s, t, i, nxt)
            if i + 2 < len(tilesB):
                load_B(i + 2)

        items = tl.finalize()
        totals = dict(tl.counters)
        eng_dsems = {}
        for op in tl.ops:
            if op["dma"]:
                eng_dsems.setdefault(op["eng"], set()).add(op["dsem"])

        def replay(eng):
            def run(e):
                for it in items[eng]:
                    if it[0] == "wait":
                        e.wait_ge(sems[it[1]], it[2])
                    elif it[0] == "drain":
                        e.drain()
                    else:
                        op = it[1]
                        inst = op["fn"](e)
                        if op.get("post_drain") and op["ev"] is not None:
                            e.drain().then_inc(sems[op["ev"][0]], 1)
                        elif op["ev"] is not None:
                            step = 16 if op["dma"] else 1
                            inst.then_inc(sems[op["ev"][0]], step)
                for dsem in sorted(eng_dsems.get(eng, ())):
                    e.wait_ge(sems[dsem], totals[dsem])
            return run

        with nc.Block() as block:
            block.sync(replay("sync"))
            block.tensor(replay("pe"))
            block.scalar(replay("act"))
            block.vector(replay("dve"))
            block.gpsimd(replay("pool"))

    return nc



def _host_prep(Wq, Wkv, Wproj, w1, b1, w2, b2, nt=NT):
    f32 = np.float32
    wqT = np.ascontiguousarray(Wq.T).astype(f32, copy=False)
    wqT = wqT.reshape(D, H, HD)[:, PERM, :].reshape(D, D)
    wq_l = np.ascontiguousarray(wqT.reshape(8, 128, D).transpose(1, 0, 2).reshape(128, 8192))
    wkvT = np.ascontiguousarray(Wkv.T).astype(f32, copy=False)
    wkv_l = np.ascontiguousarray(wkvT.reshape(8, 128, 512).transpose(1, 0, 2).reshape(128, 4096))
    wpT = np.ascontiguousarray(Wproj.T).astype(f32, copy=False)
    wpT = wpT.reshape(H, HD, D)[PERM, :, :].reshape(D, D)
    wp_l = np.ascontiguousarray(wpT.reshape(8, 128, D).transpose(1, 0, 2).reshape(128, 8192))
    w1_l = np.ascontiguousarray(
        w1.reshape(8, 128, 8, 512).transpose(1, 0, 2, 3).reshape(128, 32768))
    w2_l = np.ascontiguousarray(
        w2.reshape(MFF, 128, D).transpose(1, 0, 2).reshape(128, 32768))
    b1_l = np.ascontiguousarray(b1.reshape(MFF, 128).T)
    b2_l = np.ascontiguousarray(b2[None, :])

    theta = (1.0 / 10000.0 ** (np.arange(0, HD, 2, dtype=f32) / HD)).astype(f32)
    pos = np.arange(nt * 128, dtype=f32)
    freqs = np.outer(pos, theta).astype(f32)
    cosv = np.cos(freqs)
    sinv = np.sin(freqs)
    cos_full = np.concatenate([cosv, cosv], axis=1)
    sin_s = np.concatenate([-sinv, sinv], axis=1)
    cos_l = np.ascontiguousarray(
        cos_full.reshape(nt, 128, HD).transpose(1, 0, 2).reshape(128, nt * HD)).astype(f32)
    sin_l = np.ascontiguousarray(
        sin_s.reshape(nt, 128, HD).transpose(1, 0, 2).reshape(128, nt * HD)).astype(f32)

    ident = np.eye(128, dtype=ml_dtypes.bfloat16)
    p = np.arange(128)
    m = np.arange(128)
    sel0 = (m[None, :] == (p % 64)[:, None]).astype(ml_dtypes.bfloat16)
    sel1 = (m[None, :] == 64 + (p % 64)[:, None]).astype(ml_dtypes.bfloat16)
    BIG = np.float32(-65536.0)
    mself = np.where(np.arange(128)[None, :] > np.arange(128)[:, None], BIG, 0.0)
    mold = np.where(np.arange(128)[None, :] <= np.arange(128)[:, None], BIG, 0.0)
    msk0s = mself[(p % 64), :].astype(ml_dtypes.bfloat16)
    msk1s = mself[64 + (p % 64), :].astype(ml_dtypes.bfloat16)
    msk0o = mold[(p % 64), :].astype(ml_dtypes.bfloat16)
    msk1o = mold[64 + (p % 64), :].astype(ml_dtypes.bfloat16)
    selmsk = np.ascontiguousarray(
        np.concatenate([sel0, sel1, msk0s, msk1s, msk0o, msk1o], axis=1))

    return dict(wq=wq_l, wkv=wkv_l, wp=wp_l, w1=w1_l, w2=w2_l, b1f=b1_l, b2bc=b2_l,
                cost=cos_l, sint=sin_l, ident=ident, selmsk=selmsk)


def kernel(L0_attn_x, L0_attn_Wq, L0_attn_Wkv, L0_attn_Wproj,
           L0_ffn_w1, L0_ffn_b1, L0_ffn_w2, L0_ffn_b2):
    x = np.ascontiguousarray(np.asarray(L0_attn_x, dtype=np.float32))
    B = x.shape[0]
    common = _host_prep(np.asarray(L0_attn_Wq, np.float32),
                        np.asarray(L0_attn_Wkv, np.float32),
                        np.asarray(L0_attn_Wproj, np.float32),
                        np.asarray(L0_ffn_w1, np.float32),
                        np.asarray(L0_ffn_b1, np.float32),
                        np.asarray(L0_ffn_w2, np.float32),
                        np.asarray(L0_ffn_b2, np.float32))

    nc = build_nc()
    in_maps = []
    for c in range(NCORES):
        m = dict(common)
        m["x"] = np.ascontiguousarray(x[c * SEQS:(c + 1) * SEQS])
        in_maps.append(m)
    res = run_bass_kernel_spmd(nc, in_maps, core_ids=list(range(NCORES)))
    out = np.concatenate([r["y"] for r in res.results], axis=0)
    return out
